# version 03
# baseline (speedup 1.0000x reference)
import math
import numpy as np
import ml_dtypes
from contextlib import ExitStack

import concourse.bass as bass
import concourse.mybir as mybir
from concourse.bass_utils import run_bass_kernel_spmd

F32 = mybir.dt.float32
BF16 = mybir.dt.bfloat16
AF = mybir.ActivationFunctionType
ALU = mybir.AluOpType
BF = ml_dtypes.bfloat16

D = 4096
T = 4096
TOWN = 2048
DMC = 32
HC = 32
KH = 8
FFC = 128
DS = 1.0 / math.sqrt(128.0)
GC = 0.7978845608028654
GA = 0.044715
EPS = 1e-5


class Sem:
    def __init__(self, handle):
        self.h = handle
        self.n = 0

    def bump(self, k):
        self.n += k
        return self.n

    @property
    def done(self):
        return (self, self.n)


class Rec:
    ENGINES = ("sync", "gpsimd", "tensor", "vector", "scalar")

    def __init__(self):
        self.ops = {e: [] for e in self.ENGINES}

    def emit(self, eng, make, waits=(), inc=None, inc_k=1):
        waits = tuple((s.h, v) for (s, v) in waits if v > 0)
        handle = None
        if inc is not None:
            inc.bump(inc_k)
            handle = inc.h
        self.ops[eng].append((waits, make, handle, inc_k))

    def dma(self, eng, out_ap, in_ap, sem, waits=()):
        self.emit(eng, lambda e, o=out_ap, i=in_ap: e.dma_start(o, i),
                  waits=waits, inc=sem, inc_k=16)

    def play(self, eng_name, eng):
        for waits, make, inc_h, inc_k in self.ops[eng_name]:
            for (h, v) in waits:
                eng.wait_ge(h, v)
            ins = make(eng)
            if inc_h is not None:
                ins.then_inc(inc_h, inc_k)


def build_program(debug_taps=(), phases=8, race_check=True):
    if isinstance(phases, int):
        phases = set(range(1, phases + 1))
    _ph = set(phases)

    class _P:
        def __ge__(self, k):
            return k in _ph
    phases = _P()

    nc = bass.Bass("TRN2", target_bir_lowering=False, debug=False,
                   detect_race_conditions=race_check)

    xT = nc.dram_tensor("xT", [D, T], F32, kind="ExternalInput")
    xTown = nc.dram_tensor("xTown", [D, TOWN], F32, kind="ExternalInput")
    wkX = nc.dram_tensor("wkX", [KH, 128, DMC * 128], F32, kind="ExternalInput")
    wvX = nc.dram_tensor("wvX", [KH, 128, DMC * 128], F32, kind="ExternalInput")
    wqX = nc.dram_tensor("wqX", [HC, 128, DMC * 128], F32, kind="ExternalInput")
    woX = nc.dram_tensor("woX", [16, HC, 128, 256], F32, kind="ExternalInput")
    w1X = nc.dram_tensor("w1X", [FFC, 128, DMC * 128], F32, kind="ExternalInput")
    w2X = nc.dram_tensor("w2X", [8, FFC, 128, 512], F32, kind="ExternalInput")
    b1X = nc.dram_tensor("b1X", [128, FFC], F32, kind="ExternalInput")
    b2X = nc.dram_tensor("b2X", [128, DMC], F32, kind="ExternalInput")
    bvX = nc.dram_tensor("bvX", [128, KH], F32, kind="ExternalInput")
    onesbf = nc.dram_tensor("onesbf", [128, 1], BF16, kind="ExternalInput")
    ones1f = nc.dram_tensor("ones1f", [1, 128], F32, kind="ExternalInput")
    identbf = nc.dram_tensor("identbf", [128, 128], BF16, kind="ExternalInput")

    out = nc.dram_tensor("out", [DMC, 128, TOWN], F32, kind="ExternalOutput")

    XN = nc.dram_tensor("XN", [DMC, 128, T], BF16)
    KT = nc.dram_tensor("KT", [KH, 128, T], BF16)
    VT = nc.dram_tensor("VT", [KH, 128, T], BF16)
    Qs = nc.dram_tensor("Qs", [HC, 128, TOWN], BF16)
    CTX = nc.dram_tensor("CTX", [HC, 128, TOWN], BF16)
    Ss = nc.dram_tensor("Ss", [DMC, 128, TOWN], F32)
    Hs = nc.dram_tensor("Hs", [FFC, 128, TOWN], BF16)

    rec = Rec()
    stack = ExitStack()
    with stack:
        E = stack.enter_context
        dma_sems = []

        def SS(name, n):
            g = [Sem(E(nc.semaphore(f"{name}{i}"))) for i in range(n)]
            dma_sems.extend(g)
            return g

        csem = {}
        for name in ("B", "sPE", "sDVE", "sACT", "sSt", "sExp", "sOL",
                     "sExp2", "sBc", "sDVE2", "sPool"):
            csem[name] = Sem(E(nc.semaphore(name)))
        B, sPE, sDVE, sACT, sSt, sExp = (csem[k] for k in ("B", "sPE", "sDVE", "sACT", "sSt", "sExp"))
        sOL, sExp2, sBc, sDVE2, sPool = (csem[k] for k in ("sOL", "sExp2", "sBc", "sDVE2", "sPool"))

        nbar = [0]

        def barrier():
            dwaits = [(s, s.n) for s in dma_sems if s.n > 0]
            rec.emit("sync", lambda e: e.nop(), waits=dwaits, inc=B)
            rec.emit("gpsimd", lambda e: e.nop(), waits=dwaits, inc=B)
            rec.emit("scalar", lambda e: e.nop(), waits=dwaits, inc=B)
            rec.emit("tensor", lambda e: e.nop(), inc=B)
            rec.emit("vector", lambda e: e.nop(), inc=B)
            nbar[0] += 1
            tgt = 5 * nbar[0]
            for eng in Rec.ENGINES:
                rec.emit(eng, lambda e: e.nop(), waits=[(B, tgt)])

        S_init = SS("dIni", 1)
        ones_b = E(nc.sbuf_tensor("ones_b", [128, 1], BF16))
        ones_f = E(nc.sbuf_tensor("ones_f", [1, 128], F32))
        ident = E(nc.sbuf_tensor("ident", [128, 128], BF16))
        b1s = E(nc.sbuf_tensor("b1s", [128, FFC], F32))
        b2s = E(nc.sbuf_tensor("b2s", [128, DMC], F32))
        bvs = E(nc.sbuf_tensor("bvs", [128, KH], F32))
        rec.dma("sync", ones_b[:], onesbf[:], S_init[0])
        rec.dma("sync", ones_f[:], ones1f[:], S_init[0])
        rec.dma("sync", ident[:], identbf[:], S_init[0])
        rec.dma("sync", b1s[:], b1X[:], S_init[0])
        rec.dma("sync", b2s[:], b2X[:], S_init[0])
        rec.dma("sync", bvs[:], bvX[:], S_init[0])
        init_done = S_init[0].done

        if phases >= 1:
            S_p0x = SS("p0x", 2)
            S_p0s = SS("p0s", 2)
            with ExitStack() as ph:
                P = ph.enter_context
                TB0 = 256
                NT0 = T // TB0
                xt = [P(nc.sbuf_tensor(f"p0xt{i}", [128, DMC * TB0], F32)) for i in range(2)]
                xb = [P(nc.sbuf_tensor(f"p0xb{i}", [128, DMC * TB0], BF16)) for i in range(2)]
                sq = P(nc.sbuf_tensor("p0sq", [128, DMC * TB0], BF16))
                mrow = [P(nc.sbuf_tensor(f"p0m{i}", [1, TB0], F32)) for i in range(2)]
                vrow = P(nc.sbuf_tensor("p0v", [1, TB0], F32))
                msq = P(nc.sbuf_tensor("p0msq", [1, TB0], F32))
                lrow = P(nc.sbuf_tensor("p0l", [1, TB0], F32))
                rrow = [P(nc.sbuf_tensor(f"p0r{i}", [1, TB0], F32)) for i in range(2)]
                tmp = P(nc.sbuf_tensor("p0tmp", [128, TB0], F32))
                xn = [P(nc.sbuf_tensor(f"p0xn{i}", [128, DMC * TB0], BF16)) for i in range(2)]
                sumx = [P(nc.psum_tensor(f"p0sx{i}", [1, TB0], F32)) for i in range(2)]
                sumx2 = [P(nc.psum_tensor(f"p0sx2{i}", [1, TB0], F32)) for i in range(2)]
                mb = [P(nc.psum_tensor(f"p0mb{i}", [128, TB0], F32)) for i in range(2)]
                rb = [P(nc.psum_tensor(f"p0rb{i}", [128, TB0], F32)) for i in range(2)]

                xn_end, rows_end, sq_end = {}, {}, {}
                pe_stats_end, pe_b_end, ld_end, st_end = {}, {}, {}, {}
                for i in range(NT0):
                    t0 = i * TB0
                    p = i % 2
                    w = [(sDVE, xn_end[i - 2])] if i >= 2 else []
                    for g in range(8):
                        src = xT[g * 512:(g + 1) * 512, t0:t0 + TB0].rearrange("(c p) t -> p c t", p=128)
                        dst = xt[p][:, g * 4 * TB0:(g + 1) * 4 * TB0].rearrange("p (c t) -> p c t", t=TB0)
                        rec.dma("sync", dst, src, S_p0x[p], waits=w if g == 0 else ())
                    ld_end[i] = S_p0x[p].done
                    wv = [ld_end[i]]
                    if i >= 2:
                        wv.append((sPE, pe_stats_end[i - 2]))
                    rec.emit("vector", lambda e, a=xb[p], b_=xt[p]: e.tensor_copy(a[:], b_[:]),
                             waits=wv, inc=sDVE)
                    rec.emit("vector", lambda e, b_=xb[p]: e.tensor_mul(sq[:], b_[:], b_[:]),
                             waits=[(sPE, pe_stats_end[i - 1])] if i >= 1 else [], inc=sDVE)
                    sq_end[i] = sDVE.n
                    wt = [(sDVE, sq_end[i])]
                    if i >= 2:
                        wt.append((sDVE, rows_end[i - 2]))
                    if i == 0:
                        wt.append(init_done)
                    for c in range(DMC):
                        rec.emit("tensor",
                                 lambda e, o=sumx[p], r=xb[p], c0=c: e.matmul(
                                     o[:], ones_b[:], r[:, c0 * TB0:(c0 + 1) * TB0],
                                     start=(c0 == 0), stop=(c0 == DMC - 1)),
                                 waits=wt if c == 0 else (), inc=sPE)
                    for c in range(DMC):
                        rec.emit("tensor",
                                 lambda e, o=sumx2[p], c0=c: e.matmul(
                                     o[:], ones_b[:], sq[:, c0 * TB0:(c0 + 1) * TB0],
                                     start=(c0 == 0), stop=(c0 == DMC - 1)),
                                 inc=sPE)
                    pe_stats_end[i] = sPE.n
                    rec.emit("vector", lambda e, o=mrow[p], a=sumx[p]: e.tensor_scalar_mul(o[:], a[:], 1.0 / D),
                             waits=[(sPE, pe_stats_end[i])], inc=sDVE)
                    rec.emit("vector", lambda e, a=sumx2[p]: e.tensor_scalar(
                        vrow[:], a[:], 1.0 / D, EPS, ALU.mult, ALU.add), inc=sDVE)
                    rec.emit("vector", lambda e, a=mrow[p]: e.tensor_mul(msq[:], a[:], a[:]), inc=sDVE)
                    rec.emit("vector", lambda e: e.tensor_sub(vrow[:], vrow[:], msq[:]), inc=sDVE)
                    rows_end[i] = sDVE.n
                    rec.emit("scalar", lambda e: e.activation(lrow[:], vrow[:], AF.Ln),
                             waits=[(sDVE, rows_end[i])], inc=sACT)
                    rec.emit("scalar", lambda e, o=rrow[p]: e.activation(o[:], lrow[:], AF.Exp, scale=-0.5),
                             inc=sACT)
                    cact = sACT.n
                    wtb = [(sACT, cact)]
                    if i >= 2:
                        wtb.append((sDVE, xn_end[i - 2]))
                    rec.emit("tensor", lambda e, o=mb[p], r=mrow[p]: e.matmul(o[:], ones_f[:], r[:]),
                             waits=wtb, inc=sPE)
                    rec.emit("tensor", lambda e, o=rb[p], r=rrow[p]: e.matmul(o[:], ones_f[:], r[:]),
                             inc=sPE)
                    pe_b_end[i] = sPE.n
                    wv2 = [(sPE, pe_b_end[i])]
                    if i >= 2:
                        wv2.append(st_end[i - 2])
                    for c in range(DMC):
                        rec.emit("vector",
                                 lambda e, a=xt[p], m=mb[p], c0=c: e.tensor_sub(
                                     tmp[:], a[:, c0 * TB0:(c0 + 1) * TB0], m[:]),
                                 waits=wv2 if c == 0 else (), inc=sDVE)
                        rec.emit("vector",
                                 lambda e, o=xn[p], r=rb[p], c0=c: e.tensor_mul(
                                     o[:, c0 * TB0:(c0 + 1) * TB0], tmp[:], r[:]),
                                 inc=sDVE)
                    xn_end[i] = sDVE.n
                    for g in range(8):
                        src = xn[p][:, g * 4 * TB0:(g + 1) * 4 * TB0].rearrange("p (c t) -> p c t", t=TB0)
                        dst = XN[g * 4:(g + 1) * 4, :, t0:t0 + TB0].rearrange("c p t -> p c t")
                        rec.dma("scalar", dst, src, S_p0s[p],
                                waits=[(sDVE, xn_end[i])] if g == 0 else ())
                    st_end[i] = S_p0s[p].done
            barrier()

        def kv_phase(which):
            wsrc = wkX if which == "k" else wvX
            dstT = KT if which == "k" else VT
            S_w = SS(f"{which}dw", 1)
            S_xn = SS(f"{which}dx", 2)
            S_st = SS(f"{which}ds", 4)
            with ExitStack() as ph:
                P = ph.enter_context
                TS = 512
                wres = P(nc.sbuf_tensor(f"{which}wres", [128, KH * DMC * 128], BF16))
                xns = [P(nc.sbuf_tensor(f"{which}xn{i}", [128, DMC * TS], BF16)) for i in range(2)]
                ev = [P(nc.sbuf_tensor(f"{which}ev{i}", [128, TS], BF16)) for i in range(4)]
                banks = [P(nc.psum_tensor(f"{which}b{i}", [128, TS], F32)) for i in range(4)]
                if which == "v":
                    tev = [P(nc.sbuf_tensor(f"vtv{i}", [128, TS], BF16)) for i in range(4)]
                    tbanks = [P(nc.psum_tensor(f"vt{i}", [128, TS], BF16)) for i in range(2)]

                for kf in range(KH):
                    rec.dma("gpsimd", wres[:, kf * DMC * 128:(kf + 1) * DMC * 128], wsrc[kf], S_w[0])
                w_done = S_w[0].done

                mm_end, unit_dve_end, unit_st_end, ts_mm_end = {}, {}, {}, {}
                nts = T // TS
                for ts in range(nts):
                    t0 = ts * TS
                    p = ts % 2
                    w = [(sPE, ts_mm_end[ts - 2])] if ts >= 2 else []
                    for g in range(8):
                        src = XN[g * 4:(g + 1) * 4, :, t0:t0 + TS].rearrange("c p t -> p c t")
                        dst = xns[p][:, g * 4 * TS:(g + 1) * 4 * TS].rearrange("p (c t) -> p c t", t=TS)
                        rec.dma("sync", dst, src, S_xn[p], waits=w if g == 0 else ())
                    ld = S_xn[p].done
                    for kf in range(KH):
                        u = ts * KH + kf
                        bk = banks[u % 4]
                        wt = []
                        if kf == 0:
                            wt.append(ld)
                        if u == 0:
                            wt.append(w_done)
                            wt.append(init_done)
                        if u >= 4:
                            wt.append((sDVE, unit_dve_end[u - 4]))
                        for c in range(DMC):
                            rec.emit("tensor",
                                     lambda e, o=bk, kf0=kf, c0=c, r=xns[p]: e.matmul(
                                         o[:], wres[:, (kf0 * DMC + c0) * 128:(kf0 * DMC + c0 + 1) * 128],
                                         r[:, c0 * TS:(c0 + 1) * TS],
                                         start=(c0 == 0), stop=(c0 == DMC - 1)),
                                     waits=wt if c == 0 else (), inc=sPE)
                        mm_end[u] = sPE.n
                        if which == "k":
                            wv = [(sPE, mm_end[u])]
                            if u >= 4:
                                wv.append(unit_st_end[u - 4])
                            rec.emit("vector", lambda e, o=ev[u % 4], a=bk: e.tensor_copy(o[:], a[:]),
                                     waits=wv, inc=sDVE)
                            unit_dve_end[u] = sDVE.n
                            rec.dma("scalar", dstT[kf, :, t0:t0 + TS], ev[u % 4][:], S_st[u % 4],
                                    waits=[(sDVE, unit_dve_end[u])])
                            unit_st_end[u] = S_st[u % 4].done
                        else:
                            wv = [(sPE, mm_end[u])]
                            rec.emit("vector",
                                     lambda e, o=ev[u % 4], a=bk, kf0=kf: e.tensor_scalar(
                                         o[:], a[:], bvs[:, kf0:kf0 + 1], None, ALU.add),
                                     waits=wv, inc=sDVE)
                            ev_done = sDVE.n
                            tb = tbanks[u % 2]
                            wt2 = [(sDVE, ev_done)]
                            if u >= 2:
                                wt2.append((sDVE, unit_dve_end[u - 2]))
                            for q in range(4):
                                rec.emit("tensor",
                                         lambda e, o=tb, a=ev[u % 4], q0=q: e.transpose(
                                             o[:, q0 * 128:(q0 + 1) * 128],
                                             a[:, q0 * 128:(q0 + 1) * 128], ident[:]),
                                         waits=wt2 if q == 0 else (), inc=sPE)
                            tr_end = sPE.n
                            wv2 = [(sPE, tr_end)]
                            if u >= 4:
                                wv2.append(unit_st_end[u - 4])
                            rec.emit("vector", lambda e, o=tev[u % 4], a=tb: e.tensor_copy(o[:], a[:]),
                                     waits=wv2, inc=sDVE)
                            unit_dve_end[u] = sDVE.n
                            rec.dma("scalar", dstT[kf, :, t0:t0 + TS], tev[u % 4][:], S_st[u % 4],
                                    waits=[(sDVE, unit_dve_end[u])])
                            unit_st_end[u] = S_st[u % 4].done
                    ts_mm_end[ts] = sPE.n
            barrier()

        if phases >= 2:
            kv_phase("k")
        if phases >= 3:
            kv_phase("v")

        if phases >= 4:
            S_xno = SS("xno", 1)
            S_wq = SS("wq", 2)
            S_qst = SS("qs", 4)
            with ExitStack() as ph:
                P = ph.enter_context
                xno = P(nc.sbuf_tensor("xno", [128, DMC * TOWN], BF16))
                for g in range(DMC):
                    rec.dma("sync", xno[:, g * TOWN:(g + 1) * TOWN], XN[g, :, 0:TOWN], S_xno[0])
                xno_ld = S_xno[0].done

                with ExitStack() as p3:
                    P3 = p3.enter_context
                    TS = 512
                    wqb = [P3(nc.sbuf_tensor(f"wqb{i}", [128, DMC * 128], BF16)) for i in range(2)]
                    qev = [P3(nc.sbuf_tensor(f"qev{i}", [128, TS], BF16)) for i in range(4)]
                    qbanks = [P3(nc.psum_tensor(f"qb{i}", [128, TS], F32)) for i in range(4)]
                    hid_mm_end, q_dve_end, q_st_end, wq_ld = {}, {}, {}, {}
                    for hid in range(2):
                        rec.dma("gpsimd", wqb[hid][:], wqX[hid], S_wq[hid])
                        wq_ld[hid] = S_wq[hid].done
                    for hid in range(HC):
                        for tb in range(4):
                            u = hid * 4 + tb
                            bk = qbanks[u % 4]
                            wt = []
                            if tb == 0:
                                wt.append(wq_ld[hid])
                                if hid == 0:
                                    wt.append(xno_ld)
                                    wt.append(init_done)
                            if u >= 4:
                                wt.append((sDVE, q_dve_end[u - 4]))
                            for c in range(DMC):
                                rec.emit("tensor",
                                         lambda e, o=bk, wr=wqb[hid % 2], c0=c, tb0=tb: e.matmul(
                                             o[:], wr[:, c0 * 128:(c0 + 1) * 128],
                                             xno[:, c0 * TOWN + tb0 * TS:c0 * TOWN + (tb0 + 1) * TS],
                                             start=(c0 == 0), stop=(c0 == DMC - 1)),
                                         waits=wt if c == 0 else (), inc=sPE)
                            cpe = sPE.n
                            wv = [(sPE, cpe)]
                            if u >= 4:
                                wv.append(q_st_end[u - 4])
                            rec.emit("vector", lambda e, o=qev[u % 4], a=bk: e.tensor_copy(o[:], a[:]),
                                     waits=wv, inc=sDVE)
                            q_dve_end[u] = sDVE.n
                            rec.dma("scalar", Qs[hid, :, tb * TS:(tb + 1) * TS], qev[u % 4][:],
                                    S_qst[u % 4], waits=[(sDVE, q_dve_end[u])])
                            q_st_end[u] = S_qst[u % 4].done
                        hid_mm_end[hid] = sPE.n
                        if hid + 2 < HC:
                            rec.dma("gpsimd", wqb[hid % 2][:], wqX[hid + 2], S_wq[hid % 2],
                                    waits=[(sPE, hid_mm_end[hid])])
                            wq_ld[hid + 2] = S_wq[hid % 2].done
                    p3_dve = sDVE.n
                    p3_st = [q_st_end[HC * 4 - 4 + k] for k in range(4)]
                    p3_pe = sPE.n

                if phases >= 5:
                    S_w1 = SS("w1", 2)
                    S_hst = SS("hs", 4)
                    TS = 512
                    w1b = [P(nc.sbuf_tensor(f"w1b{i}", [128, DMC * 128], BF16)) for i in range(2)]
                    ybuf = [P(nc.sbuf_tensor(f"f1y{i}", [128, TS], F32)) for i in range(4)]
                    x2buf = [P(nc.sbuf_tensor(f"f1x2{i}", [128, TS], F32)) for i in range(2)]
                    wbuf = [P(nc.sbuf_tensor(f"f1w{i}", [128, TS], F32)) for i in range(2)]
                    ubuf = [P(nc.sbuf_tensor(f"f1u{i}", [128, TS], F32)) for i in range(2)]
                    sgbuf = [P(nc.sbuf_tensor(f"f1sg{i}", [128, TS], F32)) for i in range(2)]
                    hbuf = [P(nc.sbuf_tensor(f"f1h{i}", [128, TS], BF16)) for i in range(4)]
                    fbanks = [P(nc.psum_tensor(f"f1b{i}", [128, TS], F32)) for i in range(8)]
                    f1_u_dve, f1_y_dve, f1_st, f1_mm, w1_ld = {}, {}, {}, {}, {}
                    rec.dma("gpsimd", w1b[0][:], w1X[0], S_w1[0],
                            waits=[(sPE, p3_pe), (sDVE, p3_dve)] + p3_st)
                    w1_ld[0] = S_w1[0].done
                    rec.dma("gpsimd", w1b[1][:], w1X[1], S_w1[1])
                    w1_ld[1] = S_w1[1].done
                    for ffc in range(FFC):
                        for tb in range(4):
                            u = ffc * 4 + tb
                            bk = fbanks[u % 8]
                            wt = []
                            if tb == 0:
                                wt.append(w1_ld[ffc])
                                if ffc == 0:
                                    wt.append((sDVE, p3_dve))
                                    wt.extend(p3_st)
                                    wt.append(init_done)
                            if u >= 8:
                                wt.append((sDVE, f1_y_dve[u - 8]))
                            for c in range(DMC):
                                rec.emit("tensor",
                                         lambda e, o=bk, wr=w1b[ffc % 2], c0=c, tb0=tb: e.matmul(
                                             o[:], wr[:, c0 * 128:(c0 + 1) * 128],
                                             xno[:, c0 * TOWN + tb0 * TS:c0 * TOWN + (tb0 + 1) * TS],
                                             start=(c0 == 0), stop=(c0 == DMC - 1)),
                                         waits=wt if c == 0 else (), inc=sPE)
                            cpe = sPE.n
                            yb = ybuf[u % 4]
                            wv = [(sPE, cpe)]
                            if u >= 4:
                                wv.append((sDVE, f1_u_dve[u - 4]))
                            rec.emit("vector",
                                     lambda e, o=yb, a=bk, f0=ffc: e.tensor_scalar(
                                         o[:], a[:], b1s[:, f0:f0 + 1], None, ALU.add),
                                     waits=wv, inc=sDVE)
                            f1_y_dve[u] = sDVE.n
                            rec.emit("gpsimd",
                                     lambda e, o=x2buf[u % 2], a=yb: e.tensor_mul(o[:], a[:], a[:]),
                                     waits=[(sDVE, f1_y_dve[u])], inc=sPool)
                            cp = sPool.n
                            rec.emit("vector",
                                     lambda e, o=wbuf[u % 2], a=x2buf[u % 2]: e.tensor_scalar(
                                         o[:], a[:], 2 * GC * GA, 2 * GC, ALU.mult, ALU.add),
                                     waits=[(sPool, cp)], inc=sDVE)
                            rec.emit("vector",
                                     lambda e, o=ubuf[u % 2], a=yb, b_=wbuf[u % 2]: e.tensor_mul(o[:], a[:], b_[:]),
                                     inc=sDVE)
                            cv_u = sDVE.n
                            rec.emit("scalar",
                                     lambda e, o=sgbuf[u % 2], a=ubuf[u % 2]: e.activation(o[:], a[:], AF.Sigmoid),
                                     waits=[(sDVE, cv_u)], inc=sACT)
                            ca = sACT.n
                            hv = [(sACT, ca)]
                            if u >= 4:
                                hv.append(f1_st[u - 4])
                            rec.emit("vector",
                                     lambda e, o=hbuf[u % 4], a=yb, b_=sgbuf[u % 2]: e.tensor_mul(o[:], a[:], b_[:]),
                                     waits=hv, inc=sDVE)
                            f1_u_dve[u] = sDVE.n
                            rec.dma("scalar", Hs[ffc, :, tb * TS:(tb + 1) * TS], hbuf[u % 4][:],
                                    S_hst[u % 4], waits=[(sDVE, f1_u_dve[u])])
                            f1_st[u] = S_hst[u % 4].done
                        f1_mm[ffc] = sPE.n
                        if ffc + 2 < FFC:
                            rec.dma("gpsimd", w1b[ffc % 2][:], w1X[ffc + 2], S_w1[ffc % 2],
                                    waits=[(sPE, f1_mm[ffc])])
                            w1_ld[ffc + 2] = S_w1[ffc % 2].done
            barrier()

        if phases >= 6:
            S_qg = SS("qg", 2)
            S_kv = SS("kv", 2)
            S_cst = SS("cs", 2)
            with ExitStack() as ph:
                P = ph.enter_context
                TSQ = 512
                NG = TOWN // TSQ
                qg = [P(nc.sbuf_tensor(f"qg{i}", [128, HC * TSQ], BF16)) for i in range(2)]
                kb = [P(nc.sbuf_tensor(f"p4kb{i}", [128, T], BF16)) for i in range(2)]
                vb = [P(nc.sbuf_tensor(f"p4vb{i}", [128, T], BF16)) for i in range(2)]
                ptb = [P(nc.sbuf_tensor(f"pt{i}", [128, 32 * TSQ], BF16)) for i in range(2)]
                lnrow = P(nc.sbuf_tensor("lnrow", [1, TSQ], F32))
                livrow = P(nc.sbuf_tensor("livrow", [1, TSQ], F32))
                livb = P(nc.sbuf_tensor("livb", [128, TSQ], F32))
                ctxb = [P(nc.sbuf_tensor(f"ctxb{i}", [128, TSQ], BF16)) for i in range(2)]
                stbp = [P(nc.psum_tensor(f"stbp{i}", [128, 2 * TSQ], F32)) for i in range(2)]
                lsb = P(nc.psum_tensor("lsb", [1, TSQ], F32))
                ob = P(nc.psum_tensor("ob", [128, TSQ], F32))
                lvb = P(nc.psum_tensor("lvb", [128, TSQ], F32))

                st0, exp0, ol0, ex20, bc0, dv0 = (sSt.n, sExp.n, sOL.n, sExp2.n, sBc.n, sDVE2.n)
                ctx_st = {}
                ldq = {}
                ldkv = {}
                for grp in range(NG):
                    w = [(sDVE2, dv0 + (grp - 1) * 32)] if grp >= 2 else []
                    for j in range(HC):
                        rec.dma("sync", qg[grp % 2][:, j * TSQ:(j + 1) * TSQ],
                                Qs[j, :, grp * TSQ:(grp + 1) * TSQ], S_qg[grp % 2],
                                waits=w if j == 0 else ())
                    ldq[grp] = S_qg[grp % 2].done
                    for h in range(KH):
                        hh = grp * KH + h
                        w = [(sOL, ol0 + 32 * 4 * (hh - 1))] if hh >= 2 else []
                        rec.dma("sync", kb[hh % 2][:], KT[h], S_kv[hh % 2], waits=w)
                        rec.dma("sync", vb[hh % 2][:], VT[h], S_kv[hh % 2])
                        ldkv[hh] = S_kv[hh % 2].done

                U = 0
                for grp in range(NG):
                    for h in range(KH):
                        hh = grp * KH + h
                        for jj in range(4):
                            j = h * 4 + jj
                            u = U
                            U += 1

                            def emit_lo(kc, uu=u, p0=hh % 2):
                                ew = [(sExp, exp0 + uu * 16 + kc // 2 + 1)] if kc % 2 == 0 else []
                                rec.emit("tensor",
                                         lambda e, kc0=kc, pu=uu % 2: e.matmul(
                                             lsb[:], ones_b[:], ptb[pu][:, kc0 * TSQ:(kc0 + 1) * TSQ],
                                             start=(kc0 == 0), stop=(kc0 == 31)),
                                         waits=ew)
                                wo = [(sDVE2, dv0 + uu)] if (kc == 0 and uu >= 1) else []
                                rec.emit("tensor",
                                         lambda e, kc0=kc, pp=p0, pu=uu % 2: e.matmul(
                                             ob[:], vb[pp][:, kc0 * 128:(kc0 + 1) * 128],
                                             ptb[pu][:, kc0 * TSQ:(kc0 + 1) * TSQ],
                                             start=(kc0 == 0), stop=(kc0 == 31)),
                                         waits=wo, inc=sOL)

                            for kc in range(32):
                                g2 = u * 16 + kc // 2
                                wt = [(sExp, exp0 + g2 - 1)] if (g2 >= 2 and kc % 2 == 0) else []
                                if kc == 0:
                                    if jj == 0:
                                        wt.append(ldkv[hh])
                                    if h == 0 and jj == 0:
                                        wt.append(ldq[grp])
                                    if u == 0:
                                        wt.append(init_done)
                                rec.emit("tensor",
                                         lambda e, kc0=kc, j0=j, p0=hh % 2, pq=grp % 2: e.matmul(
                                             stbp[(kc0 // 2) % 2][:, (kc0 % 2) * TSQ:(kc0 % 2 + 1) * TSQ],
                                             kb[p0][:, kc0 * 128:(kc0 + 1) * 128],
                                             qg[pq][:, j0 * TSQ:(j0 + 1) * TSQ]),
                                         waits=wt, inc=sSt)
                                if kc == 2 and u >= 1:
                                    bw = [(sExp2, ex20 + u)]
                                    if u >= 2:
                                        bw.append((sDVE2, dv0 + u - 1))
                                    rec.emit("tensor",
                                             lambda e: e.matmul(lvb[:], ones_f[:], livrow[:]),
                                             waits=bw, inc=sBc)
                                if kc >= 4:
                                    emit_lo(kc - 4)
                            for kc in range(28, 32):
                                emit_lo(kc)
                            for pp16 in range(16):
                                wa = [(sSt, st0 + u * 32 + 2 * pp16 + 2)]
                                if u >= 2:
                                    wa.append((sOL, ol0 + (u - 2) * 32 + 2 * pp16 + 2))
                                rec.emit("scalar",
                                         lambda e, p16=pp16, pu=u % 2: e.activation(
                                             ptb[pu][:, p16 * 1024:(p16 + 1) * 1024],
                                             stbp[p16 % 2][:], AF.Exp, scale=DS),
                                         waits=wa, inc=sExp)
                            rec.emit("scalar", lambda e: e.activation(lnrow[:], lsb[:], AF.Ln),
                                     waits=[(sOL, ol0 + 32 * (u + 1))], inc=sACT)
                            rec.emit("scalar", lambda e: e.activation(livrow[:], lnrow[:], AF.Exp, scale=-1.0),
                                     inc=sExp2)
                            wv = [(sBc, bc0 + u + 1), (sOL, ol0 + 32 * (u + 1))]
                            if u >= 2:
                                wv.append(ctx_st[u - 2])
                            rec.emit("vector", lambda e: e.tensor_copy(livb[:], lvb[:]),
                                     waits=wv, inc=sDVE)
                            rec.emit("vector",
                                     lambda e, p0=u % 2: e.tensor_mul(ctxb[p0][:], ob[:], livb[:]),
                                     inc=sDVE2)
                            rec.dma("gpsimd", CTX[j, :, grp * TSQ:(grp + 1) * TSQ], ctxb[u % 2][:],
                                    S_cst[u % 2], waits=[(sDVE2, sDVE2.n)])
                            ctx_st[u] = S_cst[u % 2].done
                rec.emit("tensor", lambda e: e.matmul(lvb[:], ones_f[:], livrow[:]),
                         waits=[(sExp2, ex20 + U), (sDVE2, dv0 + U - 1)], inc=sBc)
            barrier()

        if phases >= 7:
            S_ctx = SS("cx", 1)
            S_wo = SS("wo", 4)
            S_sst = SS("ssd", 2)
            with ExitStack() as ph:
                P = ph.enter_context
                ctxr = P(nc.sbuf_tensor("ctxr", [128, HC * TOWN], BF16))
                wob = [P(nc.sbuf_tensor(f"wob{i}", [128, 256], BF16)) for i in range(4)]
                sev = [P(nc.sbuf_tensor(f"sev{i}", [128, 512], F32)) for i in range(8)]
                sbanks = [P(nc.psum_tensor(f"sb{i}", [128, 512], F32)) for i in range(8)]
                for j in range(HC):
                    rec.dma("sync", ctxr[:, j * TOWN:(j + 1) * TOWN], CTX[j], S_ctx[0])
                ld = S_ctx[0].done
                wo_mm_end, wo_ld, dp_dve_end, dp_st_end = {}, {}, {}, {}
                for i in range(4):
                    rec.dma("gpsimd", wob[i][:], woX[i // HC, i % HC], S_wo[i])
                    wo_ld[i] = S_wo[i].done
                for dp in range(16):
                    for hid in range(HC):
                        i = dp * HC + hid
                        wt = [wo_ld[i]]
                        if hid == 0:
                            if dp == 0:
                                wt.append(ld)
                            else:
                                wt.append((sDVE, dp_dve_end[dp - 1]))
                        for pos in range(8):
                            ds, tsub = pos // 4, pos % 4
                            rec.emit("tensor",
                                     lambda e, b=sbanks[pos], wr=wob[i % 4], d0=ds, t0=tsub, h0=hid: e.matmul(
                                         b[:], wr[:, d0 * 128:(d0 + 1) * 128],
                                         ctxr[:, h0 * TOWN + t0 * 512:h0 * TOWN + (t0 + 1) * 512],
                                         start=(h0 == 0), stop=(h0 == HC - 1)),
                                     waits=wt if pos == 0 else (), inc=sPE)
                        wo_mm_end[i] = sPE.n
                        if i + 4 < 16 * HC:
                            i2 = i + 4
                            rec.dma("gpsimd", wob[i2 % 4][:], woX[i2 // HC, i2 % HC], S_wo[i2 % 4],
                                    waits=[(sPE, wo_mm_end[i])])
                            wo_ld[i2] = S_wo[i2 % 4].done
                    cpe = sPE.n
                    wv = [(sPE, cpe)]
                    if dp >= 1:
                        wv.append(dp_st_end[dp - 1])
                    for pos in range(8):
                        ds, tsub = pos // 4, pos % 4
                        dc = dp * 2 + ds
                        rec.emit("vector", lambda e, o=sev[pos], b=sbanks[pos]: e.tensor_copy(o[:], b[:]),
                                 waits=wv if pos == 0 else (), inc=sDVE)
                        rec.dma("scalar", Ss[dc, :, tsub * 512:(tsub + 1) * 512], sev[pos][:],
                                S_sst[dp % 2], waits=[(sDVE, sDVE.n)])
                    dp_dve_end[dp] = sDVE.n
                    dp_st_end[dp] = S_sst[dp % 2].done
            barrier()

        if phases >= 8:
            S_w2 = SS("w2", 2)
            S_hb = SS("hbd", 2)
            S_sx = SS("sx", 2)
            S_out = SS("ou", 2)
            with ExitStack() as ph:
                P = ph.enter_context
                acc2 = [P(nc.sbuf_tensor(f"acc{i}", [128, 4 * TOWN], F32)) for i in range(2)]
                w2b = [P(nc.sbuf_tensor(f"w2b{i}", [128, 512], BF16)) for i in range(16)]
                hb = [P(nc.sbuf_tensor(f"hb{i}", [128, TOWN], BF16)) for i in range(16)]
                f2banks = [P(nc.psum_tensor(f"f2b{i}", [128, 512], F32)) for i in range(8)]
                sld = [P(nc.sbuf_tensor(f"sld{i}", [128, 512], F32)) for i in range(2)]
                xld = [P(nc.sbuf_tensor(f"xld{i}", [128, 512], F32)) for i in range(2)]
                yb2 = P(nc.sbuf_tensor("yb2", [128, 512], F32))
                sg2 = P(nc.sbuf_tensor("sg2", [128, 512], F32))
                gb2 = P(nc.sbuf_tensor("gb2", [128, 512], F32))
                tb2 = P(nc.sbuf_tensor("tb2", [128, 512], F32))
                ob2 = [P(nc.sbuf_tensor(f"ob2{i}", [128, 512], F32)) for i in range(2)]

                prev_fg_acc = None
                prev_fg_mm = None
                pending = []
                prev_batch = None

                def make_final(k, it, cv_all, batch, pb):
                    ds, tsub = it // 4, it % 4
                    dc = k * 4 + ds
                    p = it % 2
                    tsl = slice(tsub * 512, (tsub + 1) * 512)
                    acc = acc2[k % 2]

                    def emit_one():
                        if it >= 2:
                            wl = [(sDVE, batch[it - 2])]
                        elif pb is not None:
                            wl = [(sDVE, pb[14 + it])]
                        else:
                            wl = []
                        rec.dma("scalar", sld[p][:], Ss[dc, :, tsl], S_sx[p], waits=wl)
                        rec.dma("scalar", xld[p][:], xTown[dc * 128:(dc + 1) * 128, tsl], S_sx[p])
                        lds = S_sx[p].done
                        asl = slice(ds * TOWN + tsub * 512, ds * TOWN + (tsub + 1) * 512)
                        rec.emit("vector",
                                 lambda e, s=asl, dc0=dc: e.tensor_scalar(
                                     yb2[:], acc[:, s], b2s[:, dc0:dc0 + 1], None, ALU.add),
                                 waits=[lds, (sDVE, cv_all), init_done], inc=sDVE)
                        cv = sDVE.n
                        rec.emit("scalar", lambda e: e.activation(sg2[:], yb2[:], AF.Sigmoid),
                                 waits=[(sDVE, cv)], inc=sACT)
                        ca = sACT.n
                        rec.emit("vector", lambda e: e.tensor_mul(gb2[:], yb2[:], sg2[:]),
                                 waits=[(sACT, ca)], inc=sDVE)
                        rec.emit("vector", lambda e, p0=p: e.tensor_mul(tb2[:], gb2[:], sld[p0][:]), inc=sDVE)
                        wo = [(S_out[p], S_out[p].n)]
                        rec.emit("vector", lambda e, p0=p: e.tensor_add(ob2[p0][:], tb2[:], xld[p0][:]),
                                 waits=wo, inc=sDVE)
                        batch[it] = sDVE.n
                        rec.dma("scalar", out[dc, :, tsl], ob2[p][:], S_out[p],
                                waits=[(sDVE, sDVE.n)])
                    return emit_one

                for k in range(8):
                    fg_mm_end, fg_acc, w2_ld, hb_ld = {}, {}, {}, {}
                    for fg in range(2):
                        qgate = [(sPE, prev_fg_mm[14 + fg])] if k >= 1 else []
                        for fi in range(8):
                            ffc = fg * 8 + fi
                            slot = (fg % 2) * 8 + fi
                            rec.dma("gpsimd", w2b[slot][:], w2X[k, ffc], S_w2[fg % 2],
                                    waits=qgate if fi == 0 else ())
                            rec.dma("sync", hb[slot][:], Hs[ffc], S_hb[fg % 2],
                                    waits=qgate if fi == 0 else ())
                        w2_ld[fg] = S_w2[fg % 2].done
                        hb_ld[fg] = S_hb[fg % 2].done
                    for fg in range(16):
                        wt = [w2_ld[fg], hb_ld[fg]]
                        for pos in range(16):
                            ds, tsub = pos // 4, pos % 4
                            bk = f2banks[pos % 8]
                            wp = list(wt) if pos == 0 else []
                            if fg > 0 or pos >= 8:
                                prev = (fg, pos - 8) if pos >= 8 else (fg - 1, pos + 8)
                                wp.append((sDVE, fg_acc[prev]))
                            elif k >= 1:
                                wp.append((sDVE, prev_fg_acc[(15, pos + 8)]))
                            for fi in range(8):
                                slot = (fg % 2) * 8 + fi
                                rec.emit("tensor",
                                         lambda e, b=bk, wr=w2b[slot], h_=hb[slot], d0=ds, t0=tsub, f0=fi: e.matmul(
                                             b[:], wr[:, d0 * 128:(d0 + 1) * 128],
                                             h_[:, t0 * 512:(t0 + 1) * 512],
                                             start=(f0 == 0), stop=(f0 == 7)),
                                         waits=wp if fi == 0 else (), inc=sPE)
                            cpe = sPE.n
                            a = acc2[k % 2]
                            sl = slice(ds * TOWN + tsub * 512, ds * TOWN + (tsub + 1) * 512)
                            if fg == 0:
                                rec.emit("vector",
                                         lambda e, b=bk, s=sl, a0=a: e.tensor_copy(a0[:, s], b[:]),
                                         waits=[(sPE, cpe)], inc=sDVE)
                            else:
                                rec.emit("vector",
                                         lambda e, b=bk, s=sl, a0=a: e.tensor_add(a0[:, s], a0[:, s], b[:]),
                                         waits=[(sPE, cpe)], inc=sDVE)
                            fg_acc[(fg, pos)] = sDVE.n
                        fg_mm_end[fg] = sPE.n
                        if pending:
                            pending.pop(0)()
                        if fg + 2 < 16:
                            gw = [(sPE, fg_mm_end[fg])]
                            for fi in range(8):
                                ffc = (fg + 2) * 8 + fi
                                slot = (fg % 2) * 8 + fi
                                rec.dma("gpsimd", w2b[slot][:], w2X[k, ffc], S_w2[fg % 2],
                                        waits=gw if fi == 0 else ())
                                rec.dma("sync", hb[slot][:], Hs[ffc], S_hb[fg % 2],
                                        waits=gw if fi == 0 else ())
                            w2_ld[fg + 2] = S_w2[fg % 2].done
                            hb_ld[fg + 2] = S_hb[fg % 2].done
                    cv_all = sDVE.n
                    batch = {}
                    pending = [make_final(k, it, cv_all, batch, prev_batch) for it in range(16)]
                    prev_batch = batch
                    prev_fg_acc = fg_acc
                    prev_fg_mm = fg_mm_end
                for fn in pending:
                    fn()
            barrier()

        S_tap = SS("tp", 1)
        taps = {"XN": XN, "KT": KT, "VT": VT, "Q": Qs, "CTX": CTX, "S": Ss, "H": Hs}
        for name in debug_taps:
            src = taps[name]
            dt = nc.dram_tensor(f"tap_{name}", list(src.shape), src.dtype, kind="ExternalOutput")
            for i in range(src.shape[0]):
                rec.dma("sync", dt[i], src[i], S_tap[0])
        fin = [(s, s.n) for s in dma_sems if s.n > 0]
        rec.emit("sync", lambda e: e.nop(), waits=fin)

        with nc.Block() as block:
            @block.sync
            def _(e):
                rec.play("sync", e)

            @block.gpsimd
            def _(e):
                rec.play("gpsimd", e)

            @block.tensor
            def _(e):
                rec.play("tensor", e)

            @block.vector
            def _(e):
                rec.play("vector", e)

            @block.scalar
            def _(e):
                rec.play("scalar", e)

    return nc


def host_inputs(full):
    x = np.asarray(full["L0_attn_hidden_states"])
    Wq = np.asarray(full["L0_attn_Wq"])
    Wk = np.asarray(full["L0_attn_Wk"])
    Wv = np.asarray(full["L0_attn_Wv"])
    bv = np.asarray(full["L0_attn_bv"])
    Wo = np.asarray(full["L0_attn_Wo"])
    W1 = np.asarray(full["L0_ffn_W1"])
    b1 = np.asarray(full["L0_ffn_fc1_bias"])
    W2 = np.asarray(full["L0_ffn_W2"])
    b2 = np.asarray(full["L0_ffn_fc2_bias"])

    wkX = np.ascontiguousarray(Wk.reshape(KH, 128, DMC, 128).transpose(0, 3, 2, 1)).reshape(KH, 128, DMC * 128)
    wvX = np.ascontiguousarray(Wv.reshape(KH, 128, DMC, 128).transpose(0, 3, 2, 1)).reshape(KH, 128, DMC * 128)
    wqX = np.ascontiguousarray(Wq.reshape(HC, 128, DMC, 128).transpose(0, 3, 2, 1)).reshape(HC, 128, DMC * 128)
    woX = np.ascontiguousarray(Wo.reshape(16, 256, HC, 128).transpose(0, 2, 3, 1))
    w1X = np.ascontiguousarray(W1.reshape(DMC, 128, FFC, 128).transpose(2, 1, 0, 3)).reshape(FFC, 128, DMC * 128)
    w2X = np.ascontiguousarray(W2.reshape(FFC, 128, 8, 512).transpose(2, 0, 1, 3))

    shared = {
        "wkX": wkX, "wvX": wvX, "wqX": wqX, "woX": woX, "w1X": w1X, "w2X": w2X,
        "b1X": np.ascontiguousarray(b1.reshape(FFC, 128).T),
        "b2X": np.ascontiguousarray(b2.reshape(DMC, 128).T),
        "bvX": np.ascontiguousarray(bv.reshape(KH, 128).T),
        "onesbf": np.ones((128, 1), dtype=BF),
        "ones1f": np.ones((1, 128), dtype=np.float32),
        "identbf": np.eye(128, dtype=BF),
    }
    per_core = []
    for c in range(8):
        b, own = c // 2, c % 2
        xb = np.ascontiguousarray(x[b].T)
        if own == 1:
            xb = np.ascontiguousarray(np.concatenate([xb[:, TOWN:], xb[:, :TOWN]], axis=1))
        m = dict(shared)
        m["xT"] = xb
        m["xTown"] = np.ascontiguousarray(xb[:, 0:TOWN])
        per_core.append(m)
    return per_core


def kernel(**inputs):
    per_core = host_inputs(inputs)
    nc = build_program()
    res = run_bass_kernel_spmd(nc, per_core, core_ids=list(range(8)))
    outs = []
    for c in range(8):
        o = np.asarray(res.results[c]["out"])
        outs.append(o.reshape(D, TOWN))
    full = np.empty((4, T, D), dtype=np.float32)
    for b in range(4):
        oT = np.concatenate([outs[2 * b], outs[2 * b + 1]], axis=1)
        full[b] = oT.T
    return full


# version 04
# speedup vs baseline: 1.0645x; 1.0645x over previous
import math
import numpy as np
import ml_dtypes
from contextlib import ExitStack

import concourse.bass as bass
import concourse.mybir as mybir
from concourse.bass_utils import run_bass_kernel_spmd

F32 = mybir.dt.float32
BF16 = mybir.dt.bfloat16
AF = mybir.ActivationFunctionType
ALU = mybir.AluOpType
BF = ml_dtypes.bfloat16

D = 4096
T = 4096
TOWN = 2048
DMC = 32
HC = 32
KH = 8
FFC = 128
DS = 1.0 / math.sqrt(128.0)
GC = 0.7978845608028654
GA = 0.044715
EPS = 1e-5


class Sem:
    def __init__(self, handle):
        self.h = handle
        self.n = 0

    def bump(self, k):
        self.n += k
        return self.n

    @property
    def done(self):
        return (self, self.n)


class Rec:
    ENGINES = ("sync", "gpsimd", "tensor", "vector", "scalar")

    def __init__(self):
        self.ops = {e: [] for e in self.ENGINES}

    def emit(self, eng, make, waits=(), inc=None, inc_k=1):
        waits = tuple((s.h, v) for (s, v) in waits if v > 0)
        handle = None
        if inc is not None:
            inc.bump(inc_k)
            handle = inc.h
        self.ops[eng].append((waits, make, handle, inc_k))

    def dma(self, eng, out_ap, in_ap, sem, waits=()):
        self.emit(eng, lambda e, o=out_ap, i=in_ap: e.dma_start(o, i),
                  waits=waits, inc=sem, inc_k=16)

    def play(self, eng_name, eng):
        for waits, make, inc_h, inc_k in self.ops[eng_name]:
            for (h, v) in waits:
                eng.wait_ge(h, v)
            ins = make(eng)
            if inc_h is not None:
                ins.then_inc(inc_h, inc_k)


def build_program(debug_taps=(), phases=8, race_check=True):
    if isinstance(phases, int):
        phases = set(range(1, phases + 1))
    _ph = set(phases)

    class _P:
        def __ge__(self, k):
            return k in _ph
    phases = _P()

    nc = bass.Bass("TRN2", target_bir_lowering=False, debug=False,
                   detect_race_conditions=race_check)

    xT = nc.dram_tensor("xT", [D, T], F32, kind="ExternalInput")
    xTown = nc.dram_tensor("xTown", [D, TOWN], F32, kind="ExternalInput")
    wkX = nc.dram_tensor("wkX", [KH, 128, DMC * 128], F32, kind="ExternalInput")
    wvX = nc.dram_tensor("wvX", [KH, 128, DMC * 128], F32, kind="ExternalInput")
    wqX = nc.dram_tensor("wqX", [HC, 128, DMC * 128], F32, kind="ExternalInput")
    woX = nc.dram_tensor("woX", [16, HC, 128, 256], F32, kind="ExternalInput")
    w1X = nc.dram_tensor("w1X", [FFC, 128, DMC * 128], F32, kind="ExternalInput")
    w2X = nc.dram_tensor("w2X", [8, FFC, 128, 512], F32, kind="ExternalInput")
    b1X = nc.dram_tensor("b1X", [128, FFC], F32, kind="ExternalInput")
    b2X = nc.dram_tensor("b2X", [128, DMC], F32, kind="ExternalInput")
    bvX = nc.dram_tensor("bvX", [128, KH], F32, kind="ExternalInput")
    onesbf = nc.dram_tensor("onesbf", [128, 1], BF16, kind="ExternalInput")
    ones1f = nc.dram_tensor("ones1f", [1, 128], F32, kind="ExternalInput")
    identbf = nc.dram_tensor("identbf", [128, 128], BF16, kind="ExternalInput")

    out = nc.dram_tensor("out", [DMC, 128, TOWN], F32, kind="ExternalOutput")

    XN = nc.dram_tensor("XN", [DMC, 128, T], BF16)
    KT = nc.dram_tensor("KT", [KH, 128, T], BF16)
    VT = nc.dram_tensor("VT", [KH, 128, T], BF16)
    Qs = nc.dram_tensor("Qs", [HC, 128, TOWN], BF16)
    CTX = nc.dram_tensor("CTX", [HC, 128, TOWN], BF16)
    Ss = nc.dram_tensor("Ss", [DMC, 128, TOWN], F32)
    Hs = nc.dram_tensor("Hs", [FFC, 128, TOWN], BF16)

    rec = Rec()
    stack = ExitStack()
    with stack:
        E = stack.enter_context
        dma_sems = []

        def SS(name, n):
            g = [Sem(E(nc.semaphore(f"{name}{i}"))) for i in range(n)]
            dma_sems.extend(g)
            return g

        csem = {}
        for name in ("B", "sPE", "sDVE", "sACT", "sSt", "sExp", "sOL",
                     "sExp2", "sBc", "sDVE2", "sPool"):
            csem[name] = Sem(E(nc.semaphore(name)))
        B, sPE, sDVE, sACT, sSt, sExp = (csem[k] for k in ("B", "sPE", "sDVE", "sACT", "sSt", "sExp"))
        sOL, sExp2, sBc, sDVE2, sPool = (csem[k] for k in ("sOL", "sExp2", "sBc", "sDVE2", "sPool"))

        nbar = [0]

        def barrier():
            dwaits = [(s, s.n) for s in dma_sems if s.n > 0]
            rec.emit("sync", lambda e: e.nop(), waits=dwaits, inc=B)
            rec.emit("gpsimd", lambda e: e.nop(), waits=dwaits, inc=B)
            rec.emit("scalar", lambda e: e.nop(), waits=dwaits, inc=B)
            rec.emit("tensor", lambda e: e.nop(), inc=B)
            rec.emit("vector", lambda e: e.nop(), inc=B)
            nbar[0] += 1
            tgt = 5 * nbar[0]
            for eng in Rec.ENGINES:
                rec.emit(eng, lambda e: e.nop(), waits=[(B, tgt)])

        S_init = SS("dIni", 1)
        ones_b = E(nc.sbuf_tensor("ones_b", [128, 1], BF16))
        ones_f = E(nc.sbuf_tensor("ones_f", [1, 128], F32))
        ident = E(nc.sbuf_tensor("ident", [128, 128], BF16))
        b1s = E(nc.sbuf_tensor("b1s", [128, FFC], F32))
        b2s = E(nc.sbuf_tensor("b2s", [128, DMC], F32))
        bvs = E(nc.sbuf_tensor("bvs", [128, KH], F32))
        rec.dma("sync", ones_b[:], onesbf[:], S_init[0])
        rec.dma("sync", ones_f[:], ones1f[:], S_init[0])
        rec.dma("sync", ident[:], identbf[:], S_init[0])
        rec.dma("sync", b1s[:], b1X[:], S_init[0])
        rec.dma("sync", b2s[:], b2X[:], S_init[0])
        rec.dma("sync", bvs[:], bvX[:], S_init[0])
        init_done = S_init[0].done

        if phases >= 1:
            S_p0x = SS("p0x", 2)
            S_p0s = SS("p0s", 2)
            with ExitStack() as ph:
                P = ph.enter_context
                TB0 = 256
                NT0 = T // TB0
                xt = [P(nc.sbuf_tensor(f"p0xt{i}", [128, DMC * TB0], F32)) for i in range(2)]
                xb = [P(nc.sbuf_tensor(f"p0xb{i}", [128, DMC * TB0], BF16)) for i in range(2)]
                sq = P(nc.sbuf_tensor("p0sq", [128, DMC * TB0], BF16))
                mrow = [P(nc.sbuf_tensor(f"p0m{i}", [1, TB0], F32)) for i in range(2)]
                vrow = P(nc.sbuf_tensor("p0v", [1, TB0], F32))
                msq = P(nc.sbuf_tensor("p0msq", [1, TB0], F32))
                lrow = P(nc.sbuf_tensor("p0l", [1, TB0], F32))
                rrow = [P(nc.sbuf_tensor(f"p0r{i}", [1, TB0], F32)) for i in range(2)]
                tmpb = P(nc.sbuf_tensor("p0tmpb", [128, DMC * TB0], BF16))
                mbB = [P(nc.sbuf_tensor(f"p0mbB{i}", [128, TB0], BF16)) for i in range(2)]
                rbB = [P(nc.sbuf_tensor(f"p0rbB{i}", [128, TB0], BF16)) for i in range(2)]
                xn = [P(nc.sbuf_tensor(f"p0xn{i}", [128, DMC * TB0], BF16)) for i in range(2)]
                sumx = [P(nc.psum_tensor(f"p0sx{i}", [1, TB0], F32)) for i in range(2)]
                sumx2 = [P(nc.psum_tensor(f"p0sx2{i}", [1, TB0], F32)) for i in range(2)]
                mb = [P(nc.psum_tensor(f"p0mb{i}", [128, TB0], F32)) for i in range(2)]
                rb = [P(nc.psum_tensor(f"p0rb{i}", [128, TB0], F32)) for i in range(2)]

                xn_end, rows_end, sq_end = {}, {}, {}
                pe_stats_end, pe_b_end, ld_end, st_end = {}, {}, {}, {}
                for i in range(NT0):
                    t0 = i * TB0
                    p = i % 2
                    w = [(sDVE, xn_end[i - 2])] if i >= 2 else []
                    for g in range(8):
                        src = xT[g * 512:(g + 1) * 512, t0:t0 + TB0].rearrange("(c p) t -> p c t", p=128)
                        dst = xt[p][:, g * 4 * TB0:(g + 1) * 4 * TB0].rearrange("p (c t) -> p c t", t=TB0)
                        rec.dma("sync", dst, src, S_p0x[p], waits=w if g == 0 else ())
                    ld_end[i] = S_p0x[p].done
                    wv = [ld_end[i]]
                    if i >= 2:
                        wv.append((sPE, pe_stats_end[i - 2]))
                    rec.emit("gpsimd", lambda e, a=xb[p], b_=xt[p]: e.tensor_copy(a[:], b_[:]),
                             waits=wv, inc=sPool)
                    cpool = sPool.n
                    wsq = [(sPool, cpool)]
                    if i >= 1:
                        wsq.append((sPE, pe_stats_end[i - 1]))
                    rec.emit("vector", lambda e, b_=xb[p]: e.tensor_mul(sq[:], b_[:], b_[:]),
                             waits=wsq, inc=sDVE)
                    sq_end[i] = sDVE.n
                    wt = [(sDVE, sq_end[i]), (sPool, cpool)]
                    if i >= 2:
                        wt.append((sDVE, rows_end[i - 2]))
                    if i == 0:
                        wt.append(init_done)
                    for c in range(DMC):
                        rec.emit("tensor",
                                 lambda e, o=sumx[p], r=xb[p], c0=c: e.matmul(
                                     o[:], ones_b[:], r[:, c0 * TB0:(c0 + 1) * TB0],
                                     start=(c0 == 0), stop=(c0 == DMC - 1)),
                                 waits=wt if c == 0 else (), inc=sPE)
                    for c in range(DMC):
                        rec.emit("tensor",
                                 lambda e, o=sumx2[p], c0=c: e.matmul(
                                     o[:], ones_b[:], sq[:, c0 * TB0:(c0 + 1) * TB0],
                                     start=(c0 == 0), stop=(c0 == DMC - 1)),
                                 inc=sPE)
                    pe_stats_end[i] = sPE.n
                    rec.emit("vector", lambda e, o=mrow[p], a=sumx[p]: e.tensor_scalar_mul(o[:], a[:], 1.0 / D),
                             waits=[(sPE, pe_stats_end[i])], inc=sDVE)
                    rec.emit("vector", lambda e, a=sumx2[p]: e.tensor_scalar(
                        vrow[:], a[:], 1.0 / D, EPS, ALU.mult, ALU.add), inc=sDVE)
                    rec.emit("vector", lambda e, a=mrow[p]: e.tensor_mul(msq[:], a[:], a[:]), inc=sDVE)
                    rec.emit("vector", lambda e: e.tensor_sub(vrow[:], vrow[:], msq[:]), inc=sDVE)
                    rows_end[i] = sDVE.n
                    rec.emit("scalar", lambda e: e.activation(lrow[:], vrow[:], AF.Ln),
                             waits=[(sDVE, rows_end[i])], inc=sACT)
                    rec.emit("scalar", lambda e, o=rrow[p]: e.activation(o[:], lrow[:], AF.Exp, scale=-0.5),
                             inc=sACT)
                    cact = sACT.n
                    wtb = [(sACT, cact)]
                    if i >= 2:
                        wtb.append((sDVE, xn_end[i - 2]))
                    rec.emit("tensor", lambda e, o=mb[p], r=mrow[p]: e.matmul(o[:], ones_f[:], r[:]),
                             waits=wtb, inc=sPE)
                    rec.emit("tensor", lambda e, o=rb[p], r=rrow[p]: e.matmul(o[:], ones_f[:], r[:]),
                             inc=sPE)
                    pe_b_end[i] = sPE.n
                    wv2 = [(sPE, pe_b_end[i])]
                    rec.emit("vector", lambda e, o=mbB[p], m=mb[p]: e.tensor_copy(o[:], m[:]),
                             waits=wv2, inc=sDVE)
                    rec.emit("vector", lambda e, o=rbB[p], r=rb[p]: e.tensor_copy(o[:], r[:]),
                             inc=sDVE)
                    wv3 = []
                    if i >= 2:
                        wv3.append(st_end[i - 2])
                    mb_bc = bass.AP(mbB[p], 0, [[TB0, 128], [0, DMC], [1, TB0]])
                    rb_bc = bass.AP(rbB[p], 0, [[TB0, 128], [0, DMC], [1, TB0]])
                    xb3 = xb[p][:].rearrange("p (c t) -> p c t", t=TB0)
                    tmp3 = tmpb[:].rearrange("p (c t) -> p c t", t=TB0)
                    xn3 = xn[p][:].rearrange("p (c t) -> p c t", t=TB0)
                    rec.emit("vector",
                             lambda e, a=xb3, m=mb_bc, o=tmp3: e.tensor_sub(o, a, m),
                             waits=wv3, inc=sDVE)
                    rec.emit("vector",
                             lambda e, o=xn3, r=rb_bc, t3=tmp3: e.tensor_mul(o, t3, r),
                             inc=sDVE)
                    xn_end[i] = sDVE.n
                    for g in range(8):
                        src = xn[p][:, g * 4 * TB0:(g + 1) * 4 * TB0].rearrange("p (c t) -> p c t", t=TB0)
                        dst = XN[g * 4:(g + 1) * 4, :, t0:t0 + TB0].rearrange("c p t -> p c t")
                        rec.dma("scalar", dst, src, S_p0s[p],
                                waits=[(sDVE, xn_end[i])] if g == 0 else ())
                    st_end[i] = S_p0s[p].done
            barrier()

        def kv_phase(which):
            wsrc = wkX if which == "k" else wvX
            dstT = KT if which == "k" else VT
            S_w = SS(f"{which}dw", 1)
            S_xn = SS(f"{which}dx", 2)
            S_st = SS(f"{which}ds", 4)
            with ExitStack() as ph:
                P = ph.enter_context
                TS = 512
                wres = P(nc.sbuf_tensor(f"{which}wres", [128, KH * DMC * 128], BF16))
                xns = [P(nc.sbuf_tensor(f"{which}xn{i}", [128, DMC * TS], BF16)) for i in range(2)]
                ev = [P(nc.sbuf_tensor(f"{which}ev{i}", [128, TS], BF16)) for i in range(4)]
                banks = [P(nc.psum_tensor(f"{which}b{i}", [128, TS], F32)) for i in range(4)]
                if which == "v":
                    tev = [P(nc.sbuf_tensor(f"vtv{i}", [128, TS], BF16)) for i in range(4)]
                    tbanks = [P(nc.psum_tensor(f"vt{i}", [128, TS], BF16)) for i in range(2)]

                for kf in range(KH):
                    rec.dma("gpsimd", wres[:, kf * DMC * 128:(kf + 1) * DMC * 128], wsrc[kf], S_w[0])
                w_done = S_w[0].done

                mm_end, unit_dve_end, unit_st_end, ts_mm_end = {}, {}, {}, {}
                nts = T // TS
                for ts in range(nts):
                    t0 = ts * TS
                    p = ts % 2
                    w = [(sPE, ts_mm_end[ts - 2])] if ts >= 2 else []
                    for g in range(8):
                        src = XN[g * 4:(g + 1) * 4, :, t0:t0 + TS].rearrange("c p t -> p c t")
                        dst = xns[p][:, g * 4 * TS:(g + 1) * 4 * TS].rearrange("p (c t) -> p c t", t=TS)
                        rec.dma("sync", dst, src, S_xn[p], waits=w if g == 0 else ())
                    ld = S_xn[p].done
                    for kf in range(KH):
                        u = ts * KH + kf
                        bk = banks[u % 4]
                        wt = []
                        if kf == 0:
                            wt.append(ld)
                        if u == 0:
                            wt.append(w_done)
                            wt.append(init_done)
                        if u >= 4:
                            wt.append((sDVE, unit_dve_end[u - 4]))
                        for c in range(DMC):
                            rec.emit("tensor",
                                     lambda e, o=bk, kf0=kf, c0=c, r=xns[p]: e.matmul(
                                         o[:], wres[:, (kf0 * DMC + c0) * 128:(kf0 * DMC + c0 + 1) * 128],
                                         r[:, c0 * TS:(c0 + 1) * TS],
                                         start=(c0 == 0), stop=(c0 == DMC - 1)),
                                     waits=wt if c == 0 else (), inc=sPE)
                        mm_end[u] = sPE.n
                        if which == "k":
                            wv = [(sPE, mm_end[u])]
                            if u >= 4:
                                wv.append(unit_st_end[u - 4])
                            rec.emit("vector", lambda e, o=ev[u % 4], a=bk: e.tensor_copy(o[:], a[:]),
                                     waits=wv, inc=sDVE)
                            unit_dve_end[u] = sDVE.n
                            rec.dma("scalar", dstT[kf, :, t0:t0 + TS], ev[u % 4][:], S_st[u % 4],
                                    waits=[(sDVE, unit_dve_end[u])])
                            unit_st_end[u] = S_st[u % 4].done
                        else:
                            wv = [(sPE, mm_end[u])]
                            rec.emit("vector",
                                     lambda e, o=ev[u % 4], a=bk, kf0=kf: e.tensor_scalar(
                                         o[:], a[:], bvs[:, kf0:kf0 + 1], None, ALU.add),
                                     waits=wv, inc=sDVE)
                            ev_done = sDVE.n
                            tb = tbanks[u % 2]
                            wt2 = [(sDVE, ev_done)]
                            if u >= 2:
                                wt2.append((sDVE, unit_dve_end[u - 2]))
                            for q in range(4):
                                rec.emit("tensor",
                                         lambda e, o=tb, a=ev[u % 4], q0=q: e.transpose(
                                             o[:, q0 * 128:(q0 + 1) * 128],
                                             a[:, q0 * 128:(q0 + 1) * 128], ident[:]),
                                         waits=wt2 if q == 0 else (), inc=sPE)
                            tr_end = sPE.n
                            wv2 = [(sPE, tr_end)]
                            if u >= 4:
                                wv2.append(unit_st_end[u - 4])
                            rec.emit("vector", lambda e, o=tev[u % 4], a=tb: e.tensor_copy(o[:], a[:]),
                                     waits=wv2, inc=sDVE)
                            unit_dve_end[u] = sDVE.n
                            rec.dma("scalar", dstT[kf, :, t0:t0 + TS], tev[u % 4][:], S_st[u % 4],
                                    waits=[(sDVE, unit_dve_end[u])])
                            unit_st_end[u] = S_st[u % 4].done
                    ts_mm_end[ts] = sPE.n
            barrier()

        if phases >= 2:
            kv_phase("k")
        if phases >= 3:
            kv_phase("v")

        if phases >= 4:
            S_xno = SS("xno", 1)
            S_wq = SS("wq", 2)
            S_qst = SS("qs", 4)
            with ExitStack() as ph:
                P = ph.enter_context
                xno = P(nc.sbuf_tensor("xno", [128, DMC * TOWN], BF16))
                for g in range(DMC):
                    rec.dma("sync", xno[:, g * TOWN:(g + 1) * TOWN], XN[g, :, 0:TOWN], S_xno[0])
                xno_ld = S_xno[0].done

                with ExitStack() as p3:
                    P3 = p3.enter_context
                    TS = 512
                    wqb = [P3(nc.sbuf_tensor(f"wqb{i}", [128, DMC * 128], BF16)) for i in range(2)]
                    qev = [P3(nc.sbuf_tensor(f"qev{i}", [128, TS], BF16)) for i in range(4)]
                    qbanks = [P3(nc.psum_tensor(f"qb{i}", [128, TS], F32)) for i in range(4)]
                    hid_mm_end, q_dve_end, q_st_end, wq_ld = {}, {}, {}, {}
                    for hid in range(2):
                        rec.dma("gpsimd", wqb[hid][:], wqX[hid], S_wq[hid])
                        wq_ld[hid] = S_wq[hid].done
                    for hid in range(HC):
                        for tb in range(4):
                            u = hid * 4 + tb
                            bk = qbanks[u % 4]
                            wt = []
                            if tb == 0:
                                wt.append(wq_ld[hid])
                                if hid == 0:
                                    wt.append(xno_ld)
                                    wt.append(init_done)
                            if u >= 4:
                                wt.append((sDVE, q_dve_end[u - 4]))
                            for c in range(DMC):
                                rec.emit("tensor",
                                         lambda e, o=bk, wr=wqb[hid % 2], c0=c, tb0=tb: e.matmul(
                                             o[:], wr[:, c0 * 128:(c0 + 1) * 128],
                                             xno[:, c0 * TOWN + tb0 * TS:c0 * TOWN + (tb0 + 1) * TS],
                                             start=(c0 == 0), stop=(c0 == DMC - 1)),
                                         waits=wt if c == 0 else (), inc=sPE)
                            cpe = sPE.n
                            wv = [(sPE, cpe)]
                            if u >= 4:
                                wv.append(q_st_end[u - 4])
                            rec.emit("vector", lambda e, o=qev[u % 4], a=bk: e.tensor_copy(o[:], a[:]),
                                     waits=wv, inc=sDVE)
                            q_dve_end[u] = sDVE.n
                            rec.dma("scalar", Qs[hid, :, tb * TS:(tb + 1) * TS], qev[u % 4][:],
                                    S_qst[u % 4], waits=[(sDVE, q_dve_end[u])])
                            q_st_end[u] = S_qst[u % 4].done
                        hid_mm_end[hid] = sPE.n
                        if hid + 2 < HC:
                            rec.dma("gpsimd", wqb[hid % 2][:], wqX[hid + 2], S_wq[hid % 2],
                                    waits=[(sPE, hid_mm_end[hid])])
                            wq_ld[hid + 2] = S_wq[hid % 2].done
                    p3_dve = sDVE.n
                    p3_st = [q_st_end[HC * 4 - 4 + k] for k in range(4)]
                    p3_pe = sPE.n

                if phases >= 5:
                    S_w1 = SS("w1", 2)
                    S_hst = SS("hs", 4)
                    TS = 512
                    w1b = [P(nc.sbuf_tensor(f"w1b{i}", [128, DMC * 128], BF16)) for i in range(2)]
                    ybuf = [P(nc.sbuf_tensor(f"f1y{i}", [128, TS], F32)) for i in range(4)]
                    x2buf = [P(nc.sbuf_tensor(f"f1x2{i}", [128, TS], F32)) for i in range(2)]
                    wbuf = [P(nc.sbuf_tensor(f"f1w{i}", [128, TS], F32)) for i in range(2)]
                    ubuf = [P(nc.sbuf_tensor(f"f1u{i}", [128, TS], F32)) for i in range(2)]
                    sgbuf = [P(nc.sbuf_tensor(f"f1sg{i}", [128, TS], F32)) for i in range(2)]
                    hbuf = [P(nc.sbuf_tensor(f"f1h{i}", [128, TS], BF16)) for i in range(4)]
                    fbanks = [P(nc.psum_tensor(f"f1b{i}", [128, TS], F32)) for i in range(8)]
                    f1_u_dve, f1_y_dve, f1_st, f1_mm, w1_ld = {}, {}, {}, {}, {}
                    rec.dma("gpsimd", w1b[0][:], w1X[0], S_w1[0],
                            waits=[(sPE, p3_pe), (sDVE, p3_dve)] + p3_st)
                    w1_ld[0] = S_w1[0].done
                    rec.dma("gpsimd", w1b[1][:], w1X[1], S_w1[1])
                    w1_ld[1] = S_w1[1].done
                    for ffc in range(FFC):
                        for tb in range(4):
                            u = ffc * 4 + tb
                            bk = fbanks[u % 8]
                            wt = []
                            if tb == 0:
                                wt.append(w1_ld[ffc])
                                if ffc == 0:
                                    wt.append((sDVE, p3_dve))
                                    wt.extend(p3_st)
                                    wt.append(init_done)
                            if u >= 8:
                                wt.append((sDVE, f1_y_dve[u - 8]))
                            for c in range(DMC):
                                rec.emit("tensor",
                                         lambda e, o=bk, wr=w1b[ffc % 2], c0=c, tb0=tb: e.matmul(
                                             o[:], wr[:, c0 * 128:(c0 + 1) * 128],
                                             xno[:, c0 * TOWN + tb0 * TS:c0 * TOWN + (tb0 + 1) * TS],
                                             start=(c0 == 0), stop=(c0 == DMC - 1)),
                                         waits=wt if c == 0 else (), inc=sPE)
                            cpe = sPE.n
                            yb = ybuf[u % 4]
                            wv = [(sPE, cpe)]
                            if u >= 4:
                                wv.append((sDVE, f1_u_dve[u - 4]))
                            rec.emit("vector",
                                     lambda e, o=yb, a=bk, f0=ffc: e.tensor_scalar(
                                         o[:], a[:], b1s[:, f0:f0 + 1], None, ALU.add),
                                     waits=wv, inc=sDVE)
                            f1_y_dve[u] = sDVE.n
                            rec.emit("gpsimd",
                                     lambda e, o=x2buf[u % 2], a=yb: e.tensor_mul(o[:], a[:], a[:]),
                                     waits=[(sDVE, f1_y_dve[u])], inc=sPool)
                            cp = sPool.n
                            rec.emit("vector",
                                     lambda e, o=wbuf[u % 2], a=x2buf[u % 2]: e.tensor_scalar(
                                         o[:], a[:], 2 * GC * GA, 2 * GC, ALU.mult, ALU.add),
                                     waits=[(sPool, cp)], inc=sDVE)
                            rec.emit("vector",
                                     lambda e, o=ubuf[u % 2], a=yb, b_=wbuf[u % 2]: e.tensor_mul(o[:], a[:], b_[:]),
                                     inc=sDVE)
                            cv_u = sDVE.n
                            rec.emit("scalar",
                                     lambda e, o=sgbuf[u % 2], a=ubuf[u % 2]: e.activation(o[:], a[:], AF.Sigmoid),
                                     waits=[(sDVE, cv_u)], inc=sACT)
                            ca = sACT.n
                            hv = [(sACT, ca)]
                            if u >= 4:
                                hv.append(f1_st[u - 4])
                            rec.emit("vector",
                                     lambda e, o=hbuf[u % 4], a=yb, b_=sgbuf[u % 2]: e.tensor_mul(o[:], a[:], b_[:]),
                                     waits=hv, inc=sDVE)
                            f1_u_dve[u] = sDVE.n
                            rec.dma("scalar", Hs[ffc, :, tb * TS:(tb + 1) * TS], hbuf[u % 4][:],
                                    S_hst[u % 4], waits=[(sDVE, f1_u_dve[u])])
                            f1_st[u] = S_hst[u % 4].done
                        f1_mm[ffc] = sPE.n
                        if ffc + 2 < FFC:
                            rec.dma("gpsimd", w1b[ffc % 2][:], w1X[ffc + 2], S_w1[ffc % 2],
                                    waits=[(sPE, f1_mm[ffc])])
                            w1_ld[ffc + 2] = S_w1[ffc % 2].done
            barrier()

        if phases >= 6:
            S_qg = SS("qg", 2)
            S_kv = SS("kv", 2)
            S_cst = SS("cs", 2)
            with ExitStack() as ph:
                P = ph.enter_context
                TSQ = 512
                NG = TOWN // TSQ
                qg = [P(nc.sbuf_tensor(f"qg{i}", [128, HC * TSQ], BF16)) for i in range(2)]
                kb = [P(nc.sbuf_tensor(f"p4kb{i}", [128, T], BF16)) for i in range(2)]
                vb = [P(nc.sbuf_tensor(f"p4vb{i}", [128, T], BF16)) for i in range(2)]
                ptb = [P(nc.sbuf_tensor(f"pt{i}", [128, 32 * TSQ], BF16)) for i in range(2)]
                lnrow = P(nc.sbuf_tensor("lnrow", [1, TSQ], F32))
                livrow = P(nc.sbuf_tensor("livrow", [1, TSQ], F32))
                livb = P(nc.sbuf_tensor("livb", [128, TSQ], F32))
                ctxb = [P(nc.sbuf_tensor(f"ctxb{i}", [128, TSQ], BF16)) for i in range(2)]
                stbp = [P(nc.psum_tensor(f"stbp{i}", [128, 2 * TSQ], F32)) for i in range(2)]
                lsb = P(nc.psum_tensor("lsb", [1, TSQ], F32))
                ob = P(nc.psum_tensor("ob", [128, TSQ], F32))
                lvb = P(nc.psum_tensor("lvb", [128, TSQ], F32))

                st0, exp0, ol0, ex20, bc0, dv0 = (sSt.n, sExp.n, sOL.n, sExp2.n, sBc.n, sDVE2.n)
                ctx_st = {}
                ldq = {}
                ldkv = {}
                for grp in range(NG):
                    w = [(sDVE2, dv0 + (grp - 1) * 32)] if grp >= 2 else []
                    for j in range(HC):
                        rec.dma("sync", qg[grp % 2][:, j * TSQ:(j + 1) * TSQ],
                                Qs[j, :, grp * TSQ:(grp + 1) * TSQ], S_qg[grp % 2],
                                waits=w if j == 0 else ())
                    ldq[grp] = S_qg[grp % 2].done
                    for h in range(KH):
                        hh = grp * KH + h
                        w = [(sOL, ol0 + 32 * 4 * (hh - 1))] if hh >= 2 else []
                        rec.dma("sync", kb[hh % 2][:], KT[h], S_kv[hh % 2], waits=w)
                        rec.dma("sync", vb[hh % 2][:], VT[h], S_kv[hh % 2])
                        ldkv[hh] = S_kv[hh % 2].done

                U = 0
                for grp in range(NG):
                    for h in range(KH):
                        hh = grp * KH + h
                        for jj in range(4):
                            j = h * 4 + jj
                            u = U
                            U += 1

                            def emit_lo(kc, uu=u, p0=hh % 2):
                                ew = [(sExp, exp0 + uu * 16 + kc // 2 + 1)] if kc % 2 == 0 else []
                                rec.emit("tensor",
                                         lambda e, kc0=kc, pu=uu % 2: e.matmul(
                                             lsb[:], ones_b[:], ptb[pu][:, kc0 * TSQ:(kc0 + 1) * TSQ],
                                             start=(kc0 == 0), stop=(kc0 == 31)),
                                         waits=ew)
                                wo = [(sDVE2, dv0 + uu)] if (kc == 0 and uu >= 1) else []
                                rec.emit("tensor",
                                         lambda e, kc0=kc, pp=p0, pu=uu % 2: e.matmul(
                                             ob[:], vb[pp][:, kc0 * 128:(kc0 + 1) * 128],
                                             ptb[pu][:, kc0 * TSQ:(kc0 + 1) * TSQ],
                                             start=(kc0 == 0), stop=(kc0 == 31)),
                                         waits=wo, inc=sOL)

                            for kc in range(32):
                                g2 = u * 16 + kc // 2
                                wt = [(sExp, exp0 + g2 - 1)] if (g2 >= 2 and kc % 2 == 0) else []
                                if kc == 0:
                                    if jj == 0:
                                        wt.append(ldkv[hh])
                                    if h == 0 and jj == 0:
                                        wt.append(ldq[grp])
                                    if u == 0:
                                        wt.append(init_done)
                                rec.emit("tensor",
                                         lambda e, kc0=kc, j0=j, p0=hh % 2, pq=grp % 2: e.matmul(
                                             stbp[(kc0 // 2) % 2][:, (kc0 % 2) * TSQ:(kc0 % 2 + 1) * TSQ],
                                             kb[p0][:, kc0 * 128:(kc0 + 1) * 128],
                                             qg[pq][:, j0 * TSQ:(j0 + 1) * TSQ]),
                                         waits=wt, inc=sSt)
                                if kc == 2 and u >= 1:
                                    bw = [(sExp2, ex20 + u)]
                                    if u >= 2:
                                        bw.append((sDVE2, dv0 + u - 1))
                                    rec.emit("tensor",
                                             lambda e: e.matmul(lvb[:], ones_f[:], livrow[:]),
                                             waits=bw, inc=sBc)
                                if kc >= 4:
                                    emit_lo(kc - 4)
                            for kc in range(28, 32):
                                emit_lo(kc)
                            for pp16 in range(16):
                                wa = [(sSt, st0 + u * 32 + 2 * pp16 + 2)]
                                if u >= 2:
                                    wa.append((sOL, ol0 + (u - 2) * 32 + 2 * pp16 + 2))
                                rec.emit("scalar",
                                         lambda e, p16=pp16, pu=u % 2: e.activation(
                                             ptb[pu][:, p16 * 1024:(p16 + 1) * 1024],
                                             stbp[p16 % 2][:], AF.Exp, scale=DS),
                                         waits=wa, inc=sExp)
                            rec.emit("scalar", lambda e: e.activation(lnrow[:], lsb[:], AF.Ln),
                                     waits=[(sOL, ol0 + 32 * (u + 1))], inc=sACT)
                            rec.emit("scalar", lambda e: e.activation(livrow[:], lnrow[:], AF.Exp, scale=-1.0),
                                     inc=sExp2)
                            wv = [(sBc, bc0 + u + 1), (sOL, ol0 + 32 * (u + 1))]
                            if u >= 2:
                                wv.append(ctx_st[u - 2])
                            rec.emit("vector", lambda e: e.tensor_copy(livb[:], lvb[:]),
                                     waits=wv, inc=sDVE)
                            rec.emit("vector",
                                     lambda e, p0=u % 2: e.tensor_mul(ctxb[p0][:], ob[:], livb[:]),
                                     inc=sDVE2)
                            rec.dma("gpsimd", CTX[j, :, grp * TSQ:(grp + 1) * TSQ], ctxb[u % 2][:],
                                    S_cst[u % 2], waits=[(sDVE2, sDVE2.n)])
                            ctx_st[u] = S_cst[u % 2].done
                rec.emit("tensor", lambda e: e.matmul(lvb[:], ones_f[:], livrow[:]),
                         waits=[(sExp2, ex20 + U), (sDVE2, dv0 + U - 1)], inc=sBc)
            barrier()

        if phases >= 7:
            S_ctx = SS("cx", 1)
            S_wo = SS("wo", 4)
            S_sst = SS("ssd", 2)
            with ExitStack() as ph:
                P = ph.enter_context
                ctxr = P(nc.sbuf_tensor("ctxr", [128, HC * TOWN], BF16))
                wob = [P(nc.sbuf_tensor(f"wob{i}", [128, 256], BF16)) for i in range(4)]
                sev = [P(nc.sbuf_tensor(f"sev{i}", [128, 512], F32)) for i in range(8)]
                sbanks = [P(nc.psum_tensor(f"sb{i}", [128, 512], F32)) for i in range(8)]
                for j in range(HC):
                    rec.dma("sync", ctxr[:, j * TOWN:(j + 1) * TOWN], CTX[j], S_ctx[0])
                ld = S_ctx[0].done
                wo_mm_end, wo_ld, dp_dve_end, dp_st_end = {}, {}, {}, {}
                for i in range(4):
                    rec.dma("gpsimd", wob[i][:], woX[i // HC, i % HC], S_wo[i])
                    wo_ld[i] = S_wo[i].done
                for dp in range(16):
                    for hid in range(HC):
                        i = dp * HC + hid
                        wt = [wo_ld[i]]
                        if hid == 0:
                            if dp == 0:
                                wt.append(ld)
                            else:
                                wt.append((sDVE, dp_dve_end[dp - 1]))
                        for pos in range(8):
                            ds, tsub = pos // 4, pos % 4
                            rec.emit("tensor",
                                     lambda e, b=sbanks[pos], wr=wob[i % 4], d0=ds, t0=tsub, h0=hid: e.matmul(
                                         b[:], wr[:, d0 * 128:(d0 + 1) * 128],
                                         ctxr[:, h0 * TOWN + t0 * 512:h0 * TOWN + (t0 + 1) * 512],
                                         start=(h0 == 0), stop=(h0 == HC - 1)),
                                     waits=wt if pos == 0 else (), inc=sPE)
                        wo_mm_end[i] = sPE.n
                        if i + 4 < 16 * HC:
                            i2 = i + 4
                            rec.dma("gpsimd", wob[i2 % 4][:], woX[i2 // HC, i2 % HC], S_wo[i2 % 4],
                                    waits=[(sPE, wo_mm_end[i])])
                            wo_ld[i2] = S_wo[i2 % 4].done
                    cpe = sPE.n
                    wv = [(sPE, cpe)]
                    if dp >= 1:
                        wv.append(dp_st_end[dp - 1])
                    for pos in range(8):
                        ds, tsub = pos // 4, pos % 4
                        dc = dp * 2 + ds
                        rec.emit("vector", lambda e, o=sev[pos], b=sbanks[pos]: e.tensor_copy(o[:], b[:]),
                                 waits=wv if pos == 0 else (), inc=sDVE)
                        rec.dma("scalar", Ss[dc, :, tsub * 512:(tsub + 1) * 512], sev[pos][:],
                                S_sst[dp % 2], waits=[(sDVE, sDVE.n)])
                    dp_dve_end[dp] = sDVE.n
                    dp_st_end[dp] = S_sst[dp % 2].done
            barrier()

        if phases >= 8:
            S_w2 = SS("w2", 2)
            S_hb = SS("hbd", 2)
            S_sx = SS("sx", 2)
            S_out = SS("ou", 2)
            with ExitStack() as ph:
                P = ph.enter_context
                acc2 = [P(nc.sbuf_tensor(f"acc{i}", [128, 4 * TOWN], F32)) for i in range(2)]
                w2b = [P(nc.sbuf_tensor(f"w2b{i}", [128, 512], BF16)) for i in range(16)]
                hb = [P(nc.sbuf_tensor(f"hb{i}", [128, TOWN], BF16)) for i in range(16)]
                f2banks = [P(nc.psum_tensor(f"f2b{i}", [128, 512], F32)) for i in range(8)]
                sld = [P(nc.sbuf_tensor(f"sld{i}", [128, 512], F32)) for i in range(2)]
                xld = [P(nc.sbuf_tensor(f"xld{i}", [128, 512], F32)) for i in range(2)]
                yb2 = P(nc.sbuf_tensor("yb2", [128, 512], F32))
                sg2 = P(nc.sbuf_tensor("sg2", [128, 512], F32))
                gb2 = P(nc.sbuf_tensor("gb2", [128, 512], F32))
                tb2 = P(nc.sbuf_tensor("tb2", [128, 512], F32))
                ob2 = [P(nc.sbuf_tensor(f"ob2{i}", [128, 512], F32)) for i in range(2)]

                prev_fg_acc = None
                prev_fg_mm = None
                pending = []
                prev_batch = None

                def make_final(k, it, cv_all, batch, pb):
                    ds, tsub = it // 4, it % 4
                    dc = k * 4 + ds
                    p = it % 2
                    tsl = slice(tsub * 512, (tsub + 1) * 512)
                    acc = acc2[k % 2]

                    def emit_one():
                        if it >= 2:
                            wl = [(sDVE, batch[it - 2])]
                        elif pb is not None:
                            wl = [(sDVE, pb[14 + it])]
                        else:
                            wl = []
                        rec.dma("scalar", sld[p][:], Ss[dc, :, tsl], S_sx[p], waits=wl)
                        rec.dma("scalar", xld[p][:], xTown[dc * 128:(dc + 1) * 128, tsl], S_sx[p])
                        lds = S_sx[p].done
                        asl = slice(ds * TOWN + tsub * 512, ds * TOWN + (tsub + 1) * 512)
                        rec.emit("vector",
                                 lambda e, s=asl, dc0=dc: e.tensor_scalar(
                                     yb2[:], acc[:, s], b2s[:, dc0:dc0 + 1], None, ALU.add),
                                 waits=[lds, (sDVE, cv_all), init_done], inc=sDVE)
                        cv = sDVE.n
                        rec.emit("scalar", lambda e: e.activation(sg2[:], yb2[:], AF.Sigmoid),
                                 waits=[(sDVE, cv)], inc=sACT)
                        ca = sACT.n
                        rec.emit("vector", lambda e: e.tensor_mul(gb2[:], yb2[:], sg2[:]),
                                 waits=[(sACT, ca)], inc=sDVE)
                        rec.emit("vector", lambda e, p0=p: e.tensor_mul(tb2[:], gb2[:], sld[p0][:]), inc=sDVE)
                        wo = [(S_out[p], S_out[p].n)]
                        rec.emit("vector", lambda e, p0=p: e.tensor_add(ob2[p0][:], tb2[:], xld[p0][:]),
                                 waits=wo, inc=sDVE)
                        batch[it] = sDVE.n
                        rec.dma("scalar", out[dc, :, tsl], ob2[p][:], S_out[p],
                                waits=[(sDVE, sDVE.n)])
                    return emit_one

                for k in range(8):
                    fg_mm_end, fg_acc, w2_ld, hb_ld = {}, {}, {}, {}
                    for fg in range(2):
                        qgate = [(sPE, prev_fg_mm[14 + fg])] if k >= 1 else []
                        for fi in range(8):
                            ffc = fg * 8 + fi
                            slot = (fg % 2) * 8 + fi
                            rec.dma("gpsimd", w2b[slot][:], w2X[k, ffc], S_w2[fg % 2],
                                    waits=qgate if fi == 0 else ())
                            rec.dma("sync", hb[slot][:], Hs[ffc], S_hb[fg % 2],
                                    waits=qgate if fi == 0 else ())
                        w2_ld[fg] = S_w2[fg % 2].done
                        hb_ld[fg] = S_hb[fg % 2].done
                    for fg in range(16):
                        wt = [w2_ld[fg], hb_ld[fg]]
                        for pos in range(16):
                            ds, tsub = pos // 4, pos % 4
                            bk = f2banks[pos % 8]
                            wp = list(wt) if pos == 0 else []
                            if fg > 0 or pos >= 8:
                                prev = (fg, pos - 8) if pos >= 8 else (fg - 1, pos + 8)
                                wp.append((sDVE, fg_acc[prev]))
                            elif k >= 1:
                                wp.append((sDVE, prev_fg_acc[(15, pos + 8)]))
                            for fi in range(8):
                                slot = (fg % 2) * 8 + fi
                                rec.emit("tensor",
                                         lambda e, b=bk, wr=w2b[slot], h_=hb[slot], d0=ds, t0=tsub, f0=fi: e.matmul(
                                             b[:], wr[:, d0 * 128:(d0 + 1) * 128],
                                             h_[:, t0 * 512:(t0 + 1) * 512],
                                             start=(f0 == 0), stop=(f0 == 7)),
                                         waits=wp if fi == 0 else (), inc=sPE)
                            cpe = sPE.n
                            a = acc2[k % 2]
                            sl = slice(ds * TOWN + tsub * 512, ds * TOWN + (tsub + 1) * 512)
                            if fg == 0:
                                rec.emit("vector",
                                         lambda e, b=bk, s=sl, a0=a: e.tensor_copy(a0[:, s], b[:]),
                                         waits=[(sPE, cpe)], inc=sDVE)
                            else:
                                rec.emit("vector",
                                         lambda e, b=bk, s=sl, a0=a: e.tensor_add(a0[:, s], a0[:, s], b[:]),
                                         waits=[(sPE, cpe)], inc=sDVE)
                            fg_acc[(fg, pos)] = sDVE.n
                        fg_mm_end[fg] = sPE.n
                        if pending:
                            pending.pop(0)()
                        if fg + 2 < 16:
                            gw = [(sPE, fg_mm_end[fg])]
                            for fi in range(8):
                                ffc = (fg + 2) * 8 + fi
                                slot = (fg % 2) * 8 + fi
                                rec.dma("gpsimd", w2b[slot][:], w2X[k, ffc], S_w2[fg % 2],
                                        waits=gw if fi == 0 else ())
                                rec.dma("sync", hb[slot][:], Hs[ffc], S_hb[fg % 2],
                                        waits=gw if fi == 0 else ())
                            w2_ld[fg + 2] = S_w2[fg % 2].done
                            hb_ld[fg + 2] = S_hb[fg % 2].done
                    cv_all = sDVE.n
                    batch = {}
                    pending = [make_final(k, it, cv_all, batch, prev_batch) for it in range(16)]
                    prev_batch = batch
                    prev_fg_acc = fg_acc
                    prev_fg_mm = fg_mm_end
                for fn in pending:
                    fn()
            barrier()

        S_tap = SS("tp", 1)
        taps = {"XN": XN, "KT": KT, "VT": VT, "Q": Qs, "CTX": CTX, "S": Ss, "H": Hs}
        for name in debug_taps:
            src = taps[name]
            dt = nc.dram_tensor(f"tap_{name}", list(src.shape), src.dtype, kind="ExternalOutput")
            for i in range(src.shape[0]):
                rec.dma("sync", dt[i], src[i], S_tap[0])
        fin = [(s, s.n) for s in dma_sems if s.n > 0]
        rec.emit("sync", lambda e: e.nop(), waits=fin)

        with nc.Block() as block:
            @block.sync
            def _(e):
                rec.play("sync", e)

            @block.gpsimd
            def _(e):
                rec.play("gpsimd", e)

            @block.tensor
            def _(e):
                rec.play("tensor", e)

            @block.vector
            def _(e):
                rec.play("vector", e)

            @block.scalar
            def _(e):
                rec.play("scalar", e)

    return nc


def host_inputs(full):
    x = np.asarray(full["L0_attn_hidden_states"])
    Wq = np.asarray(full["L0_attn_Wq"])
    Wk = np.asarray(full["L0_attn_Wk"])
    Wv = np.asarray(full["L0_attn_Wv"])
    bv = np.asarray(full["L0_attn_bv"])
    Wo = np.asarray(full["L0_attn_Wo"])
    W1 = np.asarray(full["L0_ffn_W1"])
    b1 = np.asarray(full["L0_ffn_fc1_bias"])
    W2 = np.asarray(full["L0_ffn_W2"])
    b2 = np.asarray(full["L0_ffn_fc2_bias"])

    wkX = np.ascontiguousarray(Wk.reshape(KH, 128, DMC, 128).transpose(0, 3, 2, 1)).reshape(KH, 128, DMC * 128)
    wvX = np.ascontiguousarray(Wv.reshape(KH, 128, DMC, 128).transpose(0, 3, 2, 1)).reshape(KH, 128, DMC * 128)
    wqX = np.ascontiguousarray(Wq.reshape(HC, 128, DMC, 128).transpose(0, 3, 2, 1)).reshape(HC, 128, DMC * 128)
    woX = np.ascontiguousarray(Wo.reshape(16, 256, HC, 128).transpose(0, 2, 3, 1))
    w1X = np.ascontiguousarray(W1.reshape(DMC, 128, FFC, 128).transpose(2, 1, 0, 3)).reshape(FFC, 128, DMC * 128)
    w2X = np.ascontiguousarray(W2.reshape(FFC, 128, 8, 512).transpose(2, 0, 1, 3))

    shared = {
        "wkX": wkX, "wvX": wvX, "wqX": wqX, "woX": woX, "w1X": w1X, "w2X": w2X,
        "b1X": np.ascontiguousarray(b1.reshape(FFC, 128).T),
        "b2X": np.ascontiguousarray(b2.reshape(DMC, 128).T),
        "bvX": np.ascontiguousarray(bv.reshape(KH, 128).T),
        "onesbf": np.ones((128, 1), dtype=BF),
        "ones1f": np.ones((1, 128), dtype=np.float32),
        "identbf": np.eye(128, dtype=BF),
    }
    per_core = []
    for c in range(8):
        b, own = c // 2, c % 2
        xb = np.ascontiguousarray(x[b].T)
        if own == 1:
            xb = np.ascontiguousarray(np.concatenate([xb[:, TOWN:], xb[:, :TOWN]], axis=1))
        m = dict(shared)
        m["xT"] = xb
        m["xTown"] = np.ascontiguousarray(xb[:, 0:TOWN])
        per_core.append(m)
    return per_core


def kernel(**inputs):
    per_core = host_inputs(inputs)
    nc = build_program()
    res = run_bass_kernel_spmd(nc, per_core, core_ids=list(range(8)))
    outs = []
    for c in range(8):
        o = np.asarray(res.results[c]["out"])
        outs.append(o.reshape(D, TOWN))
    full = np.empty((4, T, D), dtype=np.float32)
    for b in range(4):
        oT = np.concatenate([outs[2 * b], outs[2 * b + 1]], axis=1)
        full[b] = oT.T
    return full


# version 05
# speedup vs baseline: 1.1411x; 1.0720x over previous
import math
import numpy as np
import ml_dtypes
from contextlib import ExitStack

import concourse.bass as bass
import concourse.mybir as mybir
from concourse.bass_utils import run_bass_kernel_spmd

F32 = mybir.dt.float32
BF16 = mybir.dt.bfloat16
AF = mybir.ActivationFunctionType
ALU = mybir.AluOpType
BF = ml_dtypes.bfloat16

D = 4096
T = 4096
TOWN = 2048
DMC = 32
HC = 32
KH = 8
FFC = 128
DS = 1.0 / math.sqrt(128.0)
GC = 0.7978845608028654
GA = 0.044715
EPS = 1e-5


class Sem:
    def __init__(self, handle):
        self.h = handle
        self.n = 0

    def bump(self, k):
        self.n += k
        return self.n

    @property
    def done(self):
        return (self, self.n)


class Rec:
    ENGINES = ("sync", "gpsimd", "tensor", "vector", "scalar")

    def __init__(self):
        self.ops = {e: [] for e in self.ENGINES}

    def emit(self, eng, make, waits=(), inc=None, inc_k=1):
        waits = tuple((s.h, v) for (s, v) in waits if v > 0)
        handle = None
        if inc is not None:
            inc.bump(inc_k)
            handle = inc.h
        self.ops[eng].append((waits, make, handle, inc_k))

    def dma(self, eng, out_ap, in_ap, sem, waits=()):
        self.emit(eng, lambda e, o=out_ap, i=in_ap: e.dma_start(o, i),
                  waits=waits, inc=sem, inc_k=16)

    def play(self, eng_name, eng):
        for waits, make, inc_h, inc_k in self.ops[eng_name]:
            for (h, v) in waits:
                eng.wait_ge(h, v)
            ins = make(eng)
            if inc_h is not None:
                ins.then_inc(inc_h, inc_k)


def build_program(debug_taps=(), phases=8, race_check=True):
    if isinstance(phases, int):
        phases = set(range(1, phases + 1))
    _ph = set(phases)

    class _P:
        def __ge__(self, k):
            return k in _ph
    phases = _P()

    nc = bass.Bass("TRN2", target_bir_lowering=False, debug=False,
                   detect_race_conditions=race_check)

    xT = nc.dram_tensor("xT", [D, T], F32, kind="ExternalInput")
    xTown = nc.dram_tensor("xTown", [D, TOWN], F32, kind="ExternalInput")
    wkX = nc.dram_tensor("wkX", [KH, 128, DMC * 128], F32, kind="ExternalInput")
    wvX = nc.dram_tensor("wvX", [KH, 128, DMC * 128], F32, kind="ExternalInput")
    wqX = nc.dram_tensor("wqX", [HC, 128, DMC * 128], F32, kind="ExternalInput")
    woX = nc.dram_tensor("woX", [16, HC, 128, 256], F32, kind="ExternalInput")
    w1X = nc.dram_tensor("w1X", [FFC, 128, DMC * 128], F32, kind="ExternalInput")
    w2X = nc.dram_tensor("w2X", [8, FFC, 128, 512], F32, kind="ExternalInput")
    b1X = nc.dram_tensor("b1X", [128, FFC], F32, kind="ExternalInput")
    b2X = nc.dram_tensor("b2X", [128, DMC], F32, kind="ExternalInput")
    bvX = nc.dram_tensor("bvX", [128, KH], F32, kind="ExternalInput")
    onesbf = nc.dram_tensor("onesbf", [128, 1], BF16, kind="ExternalInput")
    ones1f = nc.dram_tensor("ones1f", [1, 128], F32, kind="ExternalInput")
    identbf = nc.dram_tensor("identbf", [128, 128], BF16, kind="ExternalInput")

    out = nc.dram_tensor("out", [DMC, 128, TOWN], F32, kind="ExternalOutput")

    XN = nc.dram_tensor("XN", [DMC, 128, T], BF16)
    KT = nc.dram_tensor("KT", [KH, 128, T], BF16)
    VT = nc.dram_tensor("VT", [KH, 128, T], BF16)
    Qs = nc.dram_tensor("Qs", [HC, 128, TOWN], BF16)
    CTX = nc.dram_tensor("CTX", [HC, 128, TOWN], BF16)
    Ss = nc.dram_tensor("Ss", [DMC, 128, TOWN], F32)
    Hs = nc.dram_tensor("Hs", [FFC, 128, TOWN], BF16)

    rec = Rec()
    stack = ExitStack()
    with stack:
        E = stack.enter_context
        dma_sems = []

        def SS(name, n):
            g = [Sem(E(nc.semaphore(f"{name}{i}"))) for i in range(n)]
            dma_sems.extend(g)
            return g

        csem = {}
        for name in ("B", "sPE", "sDVE", "sACT", "sSt", "sExp", "sOL",
                     "sExp2", "sBc", "sDVE2", "sPool"):
            csem[name] = Sem(E(nc.semaphore(name)))
        B, sPE, sDVE, sACT, sSt, sExp = (csem[k] for k in ("B", "sPE", "sDVE", "sACT", "sSt", "sExp"))
        sOL, sExp2, sBc, sDVE2, sPool = (csem[k] for k in ("sOL", "sExp2", "sBc", "sDVE2", "sPool"))

        nbar = [0]

        def barrier():
            dwaits = [(s, s.n) for s in dma_sems if s.n > 0]
            rec.emit("sync", lambda e: e.nop(), waits=dwaits, inc=B)
            rec.emit("gpsimd", lambda e: e.nop(), waits=dwaits, inc=B)
            rec.emit("scalar", lambda e: e.nop(), waits=dwaits, inc=B)
            rec.emit("tensor", lambda e: e.nop(), inc=B)
            rec.emit("vector", lambda e: e.nop(), inc=B)
            nbar[0] += 1
            tgt = 5 * nbar[0]
            for eng in Rec.ENGINES:
                rec.emit(eng, lambda e: e.nop(), waits=[(B, tgt)])

        S_init = SS("dIni", 1)
        ones_b = E(nc.sbuf_tensor("ones_b", [128, 1], BF16))
        ones_f = E(nc.sbuf_tensor("ones_f", [1, 128], F32))
        ident = E(nc.sbuf_tensor("ident", [128, 128], BF16))
        b1s = E(nc.sbuf_tensor("b1s", [128, FFC], F32))
        b2s = E(nc.sbuf_tensor("b2s", [128, DMC], F32))
        bvs = E(nc.sbuf_tensor("bvs", [128, KH], F32))
        rec.dma("sync", ones_b[:], onesbf[:], S_init[0])
        rec.dma("sync", ones_f[:], ones1f[:], S_init[0])
        rec.dma("sync", ident[:], identbf[:], S_init[0])
        rec.dma("sync", b1s[:], b1X[:], S_init[0])
        rec.dma("sync", b2s[:], b2X[:], S_init[0])
        rec.dma("sync", bvs[:], bvX[:], S_init[0])
        init_done = S_init[0].done

        if phases >= 1:
            S_p0x = SS("p0x", 2)
            S_p0s = SS("p0s", 2)
            with ExitStack() as ph:
                P = ph.enter_context
                TB0 = 256
                NT0 = T // TB0
                xt = [P(nc.sbuf_tensor(f"p0xt{i}", [128, DMC * TB0], F32)) for i in range(2)]
                xb = [P(nc.sbuf_tensor(f"p0xb{i}", [128, DMC * TB0], BF16)) for i in range(2)]
                sq = P(nc.sbuf_tensor("p0sq", [128, DMC * TB0], BF16))
                mrow = [P(nc.sbuf_tensor(f"p0m{i}", [1, TB0], F32)) for i in range(2)]
                vrow = P(nc.sbuf_tensor("p0v", [1, TB0], F32))
                msq = P(nc.sbuf_tensor("p0msq", [1, TB0], F32))
                lrow = P(nc.sbuf_tensor("p0l", [1, TB0], F32))
                rrow = [P(nc.sbuf_tensor(f"p0r{i}", [1, TB0], F32)) for i in range(2)]
                tmpb = P(nc.sbuf_tensor("p0tmpb", [128, DMC * TB0], BF16))
                mbB = [P(nc.sbuf_tensor(f"p0mbB{i}", [128, TB0], BF16)) for i in range(2)]
                rbB = [P(nc.sbuf_tensor(f"p0rbB{i}", [128, TB0], BF16)) for i in range(2)]
                xn = [P(nc.sbuf_tensor(f"p0xn{i}", [128, DMC * TB0], BF16)) for i in range(2)]
                sumx = [P(nc.psum_tensor(f"p0sx{i}", [1, TB0], F32)) for i in range(2)]
                sumx2 = [P(nc.psum_tensor(f"p0sx2{i}", [1, TB0], F32)) for i in range(2)]
                mb = [P(nc.psum_tensor(f"p0mb{i}", [128, TB0], F32)) for i in range(2)]
                rb = [P(nc.psum_tensor(f"p0rb{i}", [128, TB0], F32)) for i in range(2)]

                xn_end, rows_end, sq_end = {}, {}, {}
                pe_stats_end, pe_b_end, ld_end, st_end = {}, {}, {}, {}
                for i in range(NT0):
                    t0 = i * TB0
                    p = i % 2
                    w = [(sDVE, xn_end[i - 2])] if i >= 2 else []
                    for g in range(8):
                        src = xT[g * 512:(g + 1) * 512, t0:t0 + TB0].rearrange("(c p) t -> p c t", p=128)
                        dst = xt[p][:, g * 4 * TB0:(g + 1) * 4 * TB0].rearrange("p (c t) -> p c t", t=TB0)
                        rec.dma("sync", dst, src, S_p0x[p], waits=w if g == 0 else ())
                    ld_end[i] = S_p0x[p].done
                    wv = [ld_end[i]]
                    if i >= 2:
                        wv.append((sPE, pe_stats_end[i - 2]))
                    rec.emit("gpsimd", lambda e, a=xb[p], b_=xt[p]: e.tensor_copy(a[:], b_[:]),
                             waits=wv, inc=sPool)
                    cpool = sPool.n
                    wsq = [(sPool, cpool)]
                    if i >= 1:
                        wsq.append((sPE, pe_stats_end[i - 1]))
                    rec.emit("vector", lambda e, b_=xb[p]: e.tensor_mul(sq[:], b_[:], b_[:]),
                             waits=wsq, inc=sDVE)
                    sq_end[i] = sDVE.n
                    wt = [(sDVE, sq_end[i]), (sPool, cpool)]
                    if i >= 2:
                        wt.append((sDVE, rows_end[i - 2]))
                    if i == 0:
                        wt.append(init_done)
                    for c in range(DMC):
                        rec.emit("tensor",
                                 lambda e, o=sumx[p], r=xb[p], c0=c: e.matmul(
                                     o[:], ones_b[:], r[:, c0 * TB0:(c0 + 1) * TB0],
                                     start=(c0 == 0), stop=(c0 == DMC - 1)),
                                 waits=wt if c == 0 else (), inc=sPE)
                    for c in range(DMC):
                        rec.emit("tensor",
                                 lambda e, o=sumx2[p], c0=c: e.matmul(
                                     o[:], ones_b[:], sq[:, c0 * TB0:(c0 + 1) * TB0],
                                     start=(c0 == 0), stop=(c0 == DMC - 1)),
                                 inc=sPE)
                    pe_stats_end[i] = sPE.n
                    rec.emit("vector", lambda e, o=mrow[p], a=sumx[p]: e.tensor_scalar_mul(o[:], a[:], 1.0 / D),
                             waits=[(sPE, pe_stats_end[i])], inc=sDVE)
                    rec.emit("vector", lambda e, a=sumx2[p]: e.tensor_scalar(
                        vrow[:], a[:], 1.0 / D, EPS, ALU.mult, ALU.add), inc=sDVE)
                    rec.emit("vector", lambda e, a=mrow[p]: e.tensor_mul(msq[:], a[:], a[:]), inc=sDVE)
                    rec.emit("vector", lambda e: e.tensor_sub(vrow[:], vrow[:], msq[:]), inc=sDVE)
                    rows_end[i] = sDVE.n
                    rec.emit("scalar", lambda e: e.activation(lrow[:], vrow[:], AF.Ln),
                             waits=[(sDVE, rows_end[i])], inc=sACT)
                    rec.emit("scalar", lambda e, o=rrow[p]: e.activation(o[:], lrow[:], AF.Exp, scale=-0.5),
                             inc=sACT)
                    cact = sACT.n
                    wtb = [(sACT, cact)]
                    if i >= 2:
                        wtb.append((sDVE, xn_end[i - 2]))
                    rec.emit("tensor", lambda e, o=mb[p], r=mrow[p]: e.matmul(o[:], ones_f[:], r[:]),
                             waits=wtb, inc=sPE)
                    rec.emit("tensor", lambda e, o=rb[p], r=rrow[p]: e.matmul(o[:], ones_f[:], r[:]),
                             inc=sPE)
                    pe_b_end[i] = sPE.n
                    wv2 = [(sPE, pe_b_end[i])]
                    rec.emit("vector", lambda e, o=mbB[p], m=mb[p]: e.tensor_copy(o[:], m[:]),
                             waits=wv2, inc=sDVE)
                    rec.emit("vector", lambda e, o=rbB[p], r=rb[p]: e.tensor_copy(o[:], r[:]),
                             inc=sDVE)
                    wv3 = []
                    if i >= 2:
                        wv3.append(st_end[i - 2])
                    mb_bc = bass.AP(mbB[p], 0, [[TB0, 128], [0, DMC], [1, TB0]])
                    rb_bc = bass.AP(rbB[p], 0, [[TB0, 128], [0, DMC], [1, TB0]])
                    xb3 = xb[p][:].rearrange("p (c t) -> p c t", t=TB0)
                    tmp3 = tmpb[:].rearrange("p (c t) -> p c t", t=TB0)
                    xn3 = xn[p][:].rearrange("p (c t) -> p c t", t=TB0)
                    rec.emit("vector",
                             lambda e, a=xb3, m=mb_bc, o=tmp3: e.tensor_sub(o, a, m),
                             waits=wv3, inc=sDVE)
                    rec.emit("vector",
                             lambda e, o=xn3, r=rb_bc, t3=tmp3: e.tensor_mul(o, t3, r),
                             inc=sDVE)
                    xn_end[i] = sDVE.n
                    for g in range(8):
                        src = xn[p][:, g * 4 * TB0:(g + 1) * 4 * TB0].rearrange("p (c t) -> p c t", t=TB0)
                        dst = XN[g * 4:(g + 1) * 4, :, t0:t0 + TB0].rearrange("c p t -> p c t")
                        rec.dma("scalar", dst, src, S_p0s[p],
                                waits=[(sDVE, xn_end[i])] if g == 0 else ())
                    st_end[i] = S_p0s[p].done
            barrier()

        def kv_phase(which):
            wsrc = wkX if which == "k" else wvX
            dstT = KT if which == "k" else VT
            S_w = SS(f"{which}dw", 1)
            S_xn = SS(f"{which}dx", 2)
            S_st = SS(f"{which}ds", 4)
            with ExitStack() as ph:
                P = ph.enter_context
                TS = 512
                wres = P(nc.sbuf_tensor(f"{which}wres", [128, KH * DMC * 128], BF16))
                xns = [P(nc.sbuf_tensor(f"{which}xn{i}", [128, DMC * TS], BF16)) for i in range(2)]
                ev = [P(nc.sbuf_tensor(f"{which}ev{i}", [128, TS], BF16)) for i in range(4)]
                banks = [P(nc.psum_tensor(f"{which}b{i}", [128, TS], F32)) for i in range(4)]
                if which == "v":
                    tev = [P(nc.sbuf_tensor(f"vtv{i}", [128, TS], BF16)) for i in range(4)]
                    tbanks = [P(nc.psum_tensor(f"vt{i}", [128, TS], BF16)) for i in range(2)]

                for kf in range(KH):
                    rec.dma("gpsimd", wres[:, kf * DMC * 128:(kf + 1) * DMC * 128], wsrc[kf], S_w[0])
                w_done = S_w[0].done

                mm_end, unit_dve_end, unit_st_end, ts_mm_end = {}, {}, {}, {}
                nts = T // TS
                for ts in range(nts):
                    t0 = ts * TS
                    p = ts % 2
                    w = [(sPE, ts_mm_end[ts - 2])] if ts >= 2 else []
                    for g in range(8):
                        src = XN[g * 4:(g + 1) * 4, :, t0:t0 + TS].rearrange("c p t -> p c t")
                        dst = xns[p][:, g * 4 * TS:(g + 1) * 4 * TS].rearrange("p (c t) -> p c t", t=TS)
                        rec.dma("sync", dst, src, S_xn[p], waits=w if g == 0 else ())
                    ld = S_xn[p].done
                    for kf in range(KH):
                        u = ts * KH + kf
                        bk = banks[u % 4]
                        wt = []
                        if kf == 0:
                            wt.append(ld)
                        if u == 0:
                            wt.append(w_done)
                            wt.append(init_done)
                        if u >= 4:
                            wt.append((sDVE, unit_dve_end[u - 4]))
                        for c in range(DMC):
                            rec.emit("tensor",
                                     lambda e, o=bk, kf0=kf, c0=c, r=xns[p]: e.matmul(
                                         o[:], wres[:, (kf0 * DMC + c0) * 128:(kf0 * DMC + c0 + 1) * 128],
                                         r[:, c0 * TS:(c0 + 1) * TS],
                                         start=(c0 == 0), stop=(c0 == DMC - 1)),
                                     waits=wt if c == 0 else (), inc=sPE)
                        mm_end[u] = sPE.n
                        if which == "k":
                            wv = [(sPE, mm_end[u])]
                            if u >= 4:
                                wv.append(unit_st_end[u - 4])
                            rec.emit("vector", lambda e, o=ev[u % 4], a=bk: e.tensor_copy(o[:], a[:]),
                                     waits=wv, inc=sDVE)
                            unit_dve_end[u] = sDVE.n
                            rec.dma("scalar", dstT[kf, :, t0:t0 + TS], ev[u % 4][:], S_st[u % 4],
                                    waits=[(sDVE, unit_dve_end[u])])
                            unit_st_end[u] = S_st[u % 4].done
                        else:
                            wv = [(sPE, mm_end[u])]
                            rec.emit("vector",
                                     lambda e, o=ev[u % 4], a=bk, kf0=kf: e.tensor_scalar(
                                         o[:], a[:], bvs[:, kf0:kf0 + 1], None, ALU.add),
                                     waits=wv, inc=sDVE)
                            ev_done = sDVE.n
                            tb = tbanks[u % 2]
                            wt2 = [(sDVE, ev_done)]
                            if u >= 2:
                                wt2.append((sDVE, unit_dve_end[u - 2]))
                            for q in range(4):
                                rec.emit("tensor",
                                         lambda e, o=tb, a=ev[u % 4], q0=q: e.transpose(
                                             o[:, q0 * 128:(q0 + 1) * 128],
                                             a[:, q0 * 128:(q0 + 1) * 128], ident[:]),
                                         waits=wt2 if q == 0 else (), inc=sPE)
                            tr_end = sPE.n
                            wv2 = [(sPE, tr_end)]
                            if u >= 4:
                                wv2.append(unit_st_end[u - 4])
                            rec.emit("vector", lambda e, o=tev[u % 4], a=tb: e.tensor_copy(o[:], a[:]),
                                     waits=wv2, inc=sDVE)
                            unit_dve_end[u] = sDVE.n
                            rec.dma("scalar", dstT[kf, :, t0:t0 + TS], tev[u % 4][:], S_st[u % 4],
                                    waits=[(sDVE, unit_dve_end[u])])
                            unit_st_end[u] = S_st[u % 4].done
                    ts_mm_end[ts] = sPE.n
            barrier()

        if phases >= 2:
            kv_phase("k")
        if phases >= 3:
            kv_phase("v")

        if phases >= 4:
            S_xno = SS("xno", 1)
            S_wq = SS("wq", 2)
            S_qst = SS("qs", 4)
            with ExitStack() as ph:
                P = ph.enter_context
                xno = P(nc.sbuf_tensor("xno", [128, DMC * TOWN], BF16))
                for g in range(DMC):
                    rec.dma("sync", xno[:, g * TOWN:(g + 1) * TOWN], XN[g, :, 0:TOWN], S_xno[0])
                xno_ld = S_xno[0].done

                with ExitStack() as p3:
                    P3 = p3.enter_context
                    TS = 512
                    wqb = [P3(nc.sbuf_tensor(f"wqb{i}", [128, DMC * 128], BF16)) for i in range(2)]
                    qev = [P3(nc.sbuf_tensor(f"qev{i}", [128, TS], BF16)) for i in range(4)]
                    qbanks = [P3(nc.psum_tensor(f"qb{i}", [128, TS], F32)) for i in range(4)]
                    hid_mm_end, q_dve_end, q_st_end, wq_ld = {}, {}, {}, {}
                    for hid in range(2):
                        rec.dma("gpsimd", wqb[hid][:], wqX[hid], S_wq[hid])
                        wq_ld[hid] = S_wq[hid].done
                    for hid in range(HC):
                        for tb in range(4):
                            u = hid * 4 + tb
                            bk = qbanks[u % 4]
                            wt = []
                            if tb == 0:
                                wt.append(wq_ld[hid])
                                if hid == 0:
                                    wt.append(xno_ld)
                                    wt.append(init_done)
                            if u >= 4:
                                wt.append((sDVE, q_dve_end[u - 4]))
                            for c in range(DMC):
                                rec.emit("tensor",
                                         lambda e, o=bk, wr=wqb[hid % 2], c0=c, tb0=tb: e.matmul(
                                             o[:], wr[:, c0 * 128:(c0 + 1) * 128],
                                             xno[:, c0 * TOWN + tb0 * TS:c0 * TOWN + (tb0 + 1) * TS],
                                             start=(c0 == 0), stop=(c0 == DMC - 1)),
                                         waits=wt if c == 0 else (), inc=sPE)
                            cpe = sPE.n
                            wv = [(sPE, cpe)]
                            if u >= 4:
                                wv.append(q_st_end[u - 4])
                            rec.emit("vector", lambda e, o=qev[u % 4], a=bk: e.tensor_copy(o[:], a[:]),
                                     waits=wv, inc=sDVE)
                            q_dve_end[u] = sDVE.n
                            rec.dma("scalar", Qs[hid, :, tb * TS:(tb + 1) * TS], qev[u % 4][:],
                                    S_qst[u % 4], waits=[(sDVE, q_dve_end[u])])
                            q_st_end[u] = S_qst[u % 4].done
                        hid_mm_end[hid] = sPE.n
                        if hid + 2 < HC:
                            rec.dma("gpsimd", wqb[hid % 2][:], wqX[hid + 2], S_wq[hid % 2],
                                    waits=[(sPE, hid_mm_end[hid])])
                            wq_ld[hid + 2] = S_wq[hid % 2].done
                    p3_dve = sDVE.n
                    p3_st = [q_st_end[HC * 4 - 4 + k] for k in range(4)]
                    p3_pe = sPE.n

                if phases >= 5:
                    S_w1 = SS("w1", 2)
                    S_hst = SS("hs", 4)
                    TS = 512
                    w1b = [P(nc.sbuf_tensor(f"w1b{i}", [128, DMC * 128], BF16)) for i in range(2)]
                    ybuf = [P(nc.sbuf_tensor(f"f1y{i}", [128, TS], F32)) for i in range(4)]
                    x2buf = [P(nc.sbuf_tensor(f"f1x2{i}", [128, TS], F32)) for i in range(2)]
                    wbuf = [P(nc.sbuf_tensor(f"f1w{i}", [128, TS], F32)) for i in range(2)]
                    ubuf = [P(nc.sbuf_tensor(f"f1u{i}", [128, TS], F32)) for i in range(2)]
                    sgbuf = [P(nc.sbuf_tensor(f"f1sg{i}", [128, TS], F32)) for i in range(2)]
                    hbuf = [P(nc.sbuf_tensor(f"f1h{i}", [128, TS], BF16)) for i in range(4)]
                    fbanks = [P(nc.psum_tensor(f"f1b{i}", [128, TS], F32)) for i in range(8)]
                    f1_u_dve, f1_y_dve, f1_st, f1_mm, w1_ld = {}, {}, {}, {}, {}
                    rec.dma("gpsimd", w1b[0][:], w1X[0], S_w1[0],
                            waits=[(sPE, p3_pe), (sDVE, p3_dve)] + p3_st)
                    w1_ld[0] = S_w1[0].done
                    rec.dma("gpsimd", w1b[1][:], w1X[1], S_w1[1])
                    w1_ld[1] = S_w1[1].done
                    for ffc in range(FFC):
                        for tb in range(4):
                            u = ffc * 4 + tb
                            bk = fbanks[u % 8]
                            wt = []
                            if tb == 0:
                                wt.append(w1_ld[ffc])
                                if ffc == 0:
                                    wt.append((sDVE, p3_dve))
                                    wt.extend(p3_st)
                                    wt.append(init_done)
                            if u >= 8:
                                wt.append((sDVE, f1_y_dve[u - 8]))
                            for c in range(DMC):
                                rec.emit("tensor",
                                         lambda e, o=bk, wr=w1b[ffc % 2], c0=c, tb0=tb: e.matmul(
                                             o[:], wr[:, c0 * 128:(c0 + 1) * 128],
                                             xno[:, c0 * TOWN + tb0 * TS:c0 * TOWN + (tb0 + 1) * TS],
                                             start=(c0 == 0), stop=(c0 == DMC - 1)),
                                         waits=wt if c == 0 else (), inc=sPE)
                            cpe = sPE.n
                            yb = ybuf[u % 4]
                            wv = [(sPE, cpe)]
                            if u >= 4:
                                wv.append((sDVE, f1_u_dve[u - 4]))
                            rec.emit("vector",
                                     lambda e, o=yb, a=bk, f0=ffc: e.tensor_scalar(
                                         o[:], a[:], b1s[:, f0:f0 + 1], None, ALU.add),
                                     waits=wv, inc=sDVE)
                            f1_y_dve[u] = sDVE.n
                            rec.emit("gpsimd",
                                     lambda e, o=x2buf[u % 2], a=yb: e.tensor_mul(o[:], a[:], a[:]),
                                     waits=[(sDVE, f1_y_dve[u])], inc=sPool)
                            cp = sPool.n
                            rec.emit("vector",
                                     lambda e, o=wbuf[u % 2], a=x2buf[u % 2]: e.tensor_scalar(
                                         o[:], a[:], 2 * GC * GA, 2 * GC, ALU.mult, ALU.add),
                                     waits=[(sPool, cp)], inc=sDVE)
                            rec.emit("vector",
                                     lambda e, o=ubuf[u % 2], a=yb, b_=wbuf[u % 2]: e.tensor_mul(o[:], a[:], b_[:]),
                                     inc=sDVE)
                            cv_u = sDVE.n
                            rec.emit("scalar",
                                     lambda e, o=sgbuf[u % 2], a=ubuf[u % 2]: e.activation(o[:], a[:], AF.Sigmoid),
                                     waits=[(sDVE, cv_u)], inc=sACT)
                            ca = sACT.n
                            hv = [(sACT, ca)]
                            if u >= 4:
                                hv.append(f1_st[u - 4])
                            rec.emit("vector",
                                     lambda e, o=hbuf[u % 4], a=yb, b_=sgbuf[u % 2]: e.tensor_mul(o[:], a[:], b_[:]),
                                     waits=hv, inc=sDVE)
                            f1_u_dve[u] = sDVE.n
                            rec.dma("scalar", Hs[ffc, :, tb * TS:(tb + 1) * TS], hbuf[u % 4][:],
                                    S_hst[u % 4], waits=[(sDVE, f1_u_dve[u])])
                            f1_st[u] = S_hst[u % 4].done
                        f1_mm[ffc] = sPE.n
                        if ffc + 2 < FFC:
                            rec.dma("gpsimd", w1b[ffc % 2][:], w1X[ffc + 2], S_w1[ffc % 2],
                                    waits=[(sPE, f1_mm[ffc])])
                            w1_ld[ffc + 2] = S_w1[ffc % 2].done
            barrier()

        if phases >= 6:
            S_qg = SS("qg", 2)
            S_kv = SS("kv", 2)
            S_cst = SS("cs", 2)
            with ExitStack() as ph:
                P = ph.enter_context
                TSQ = 512
                NG = TOWN // TSQ
                qg = [P(nc.sbuf_tensor(f"qg{i}", [128, HC * TSQ], BF16)) for i in range(2)]
                kb = [P(nc.sbuf_tensor(f"p4kb{i}", [128, T], BF16)) for i in range(2)]
                vb = [P(nc.sbuf_tensor(f"p4vb{i}", [128, T], BF16)) for i in range(2)]
                ptb = [P(nc.sbuf_tensor(f"pt{i}", [128, 32 * TSQ], BF16)) for i in range(2)]
                lnrow = P(nc.sbuf_tensor("lnrow", [1, TSQ], F32))
                livrow = P(nc.sbuf_tensor("livrow", [1, TSQ], F32))
                livb = P(nc.sbuf_tensor("livb", [128, TSQ], F32))
                ctxb = [P(nc.sbuf_tensor(f"ctxb{i}", [128, TSQ], BF16)) for i in range(2)]
                stbp = [P(nc.psum_tensor(f"stbp{i}", [128, 2 * TSQ], F32)) for i in range(2)]
                lsb = P(nc.psum_tensor("lsb", [1, TSQ], F32))
                ob = P(nc.psum_tensor("ob", [128, TSQ], F32))
                lvb = P(nc.psum_tensor("lvb", [128, TSQ], F32))

                st0, exp0, ol0, ex20, bc0, dv0 = (sSt.n, sExp.n, sOL.n, sExp2.n, sBc.n, sDVE2.n)
                ctx_st = {}
                ldq = {}
                ldkv = {}
                for grp in range(NG):
                    w = [(sDVE2, dv0 + (grp - 1) * 32)] if grp >= 2 else []
                    for j in range(HC):
                        rec.dma("sync", qg[grp % 2][:, j * TSQ:(j + 1) * TSQ],
                                Qs[j, :, grp * TSQ:(grp + 1) * TSQ], S_qg[grp % 2],
                                waits=w if j == 0 else ())
                    ldq[grp] = S_qg[grp % 2].done
                    for h in range(KH):
                        hh = grp * KH + h
                        w = [(sOL, ol0 + 32 * 4 * (hh - 1))] if hh >= 2 else []
                        rec.dma("sync", kb[hh % 2][:], KT[h], S_kv[hh % 2], waits=w)
                        rec.dma("sync", vb[hh % 2][:], VT[h], S_kv[hh % 2])
                        ldkv[hh] = S_kv[hh % 2].done

                U = 0
                units = []
                for grp in range(NG):
                    for h in range(KH):
                        hh = grp * KH + h
                        for jj in range(4):
                            units.append((U, h * 4 + jj, grp, h, hh))
                            U += 1

                def emit_lo_batch(uu, pv, pu):
                    for kc in range(32):
                        ew = [(sExp, exp0 + uu * 16 + kc // 2 + 1)] if kc % 2 == 0 else []
                        rec.emit("tensor",
                                 lambda e, kc0=kc: e.matmul(
                                     lsb[:], ones_b[:], ptb[pu][:, kc0 * TSQ:(kc0 + 1) * TSQ],
                                     start=(kc0 == 0), stop=(kc0 == 31)),
                                 waits=ew)
                    for kc in range(32):
                        wo = [(sDVE2, dv0 + uu)] if (kc == 0 and uu >= 1) else []
                        rec.emit("tensor",
                                 lambda e, kc0=kc: e.matmul(
                                     ob[:], vb[pv][:, kc0 * 128:(kc0 + 1) * 128],
                                     ptb[pu][:, kc0 * TSQ:(kc0 + 1) * TSQ],
                                     start=(kc0 == 0), stop=(kc0 == 31)),
                                 waits=wo, inc=sOL)

                prev = None
                for (u, j, grp, h, hh) in units:
                    if u >= 2:
                        bw = [(sExp2, ex20 + u - 1)]
                        if u >= 3:
                            bw.append((sDVE2, dv0 + u - 2))
                        rec.emit("tensor",
                                 lambda e: e.matmul(lvb[:], ones_f[:], livrow[:]),
                                 waits=bw, inc=sBc)
                    for kc in range(32):
                        g2 = u * 16 + kc // 2
                        wt = [(sExp, exp0 + g2 - 1)] if (g2 >= 2 and kc % 2 == 0) else []
                        if kc == 0:
                            if j % 4 == 0:
                                wt.append(ldkv[hh])
                            if h == 0 and j % 4 == 0:
                                wt.append(ldq[grp])
                            if u == 0:
                                wt.append(init_done)
                        rec.emit("tensor",
                                 lambda e, kc0=kc, j0=j, p0=hh % 2, pq=grp % 2: e.matmul(
                                     stbp[(kc0 // 2) % 2][:, (kc0 % 2) * TSQ:(kc0 % 2 + 1) * TSQ],
                                     kb[p0][:, kc0 * 128:(kc0 + 1) * 128],
                                     qg[pq][:, j0 * TSQ:(j0 + 1) * TSQ]),
                                 waits=wt, inc=sSt)
                    if prev is not None:
                        emit_lo_batch(*prev)
                    prev = (u, hh % 2, u % 2)
                    for pp16 in range(16):
                        wa = [(sSt, st0 + u * 32 + 2 * pp16 + 2)]
                        if u >= 2:
                            wa.append((sOL, ol0 + (u - 2) * 32 + 2 * pp16 + 2))
                        rec.emit("scalar",
                                 lambda e, p16=pp16, pu=u % 2: e.activation(
                                     ptb[pu][:, p16 * 1024:(p16 + 1) * 1024],
                                     stbp[p16 % 2][:], AF.Exp, scale=DS),
                                 waits=wa, inc=sExp)
                    if u >= 1:
                        rec.emit("scalar", lambda e: e.activation(lnrow[:], lsb[:], AF.Ln),
                                 waits=[(sOL, ol0 + 32 * u)], inc=sACT)
                        rec.emit("scalar", lambda e: e.activation(livrow[:], lnrow[:], AF.Exp, scale=-1.0),
                                 inc=sExp2)
                    if u >= 1:
                        u1 = u - 1
                        wv = [(sBc, bc0 + u1 + 1), (sOL, ol0 + 32 * (u1 + 1))]
                        if u1 >= 2:
                            wv.append(ctx_st[u1 - 2])
                        rec.emit("vector", lambda e: e.tensor_copy(livb[:], lvb[:]),
                                 waits=wv, inc=sDVE)
                        rec.emit("vector",
                                 lambda e, p0=u1 % 2: e.tensor_mul(ctxb[p0][:], ob[:], livb[:]),
                                 inc=sDVE2)
                        ju1 = units[u1][1]
                        gu1 = units[u1][2]
                        rec.dma("gpsimd", CTX[ju1, :, gu1 * TSQ:(gu1 + 1) * TSQ], ctxb[u1 % 2][:],
                                S_cst[u1 % 2], waits=[(sDVE2, sDVE2.n)])
                        ctx_st[u1] = S_cst[u1 % 2].done
                uL, jL, gL, hL, hhL = units[-1]
                rec.emit("tensor", lambda e: e.matmul(lvb[:], ones_f[:], livrow[:]),
                         waits=[(sExp2, ex20 + uL), (sDVE2, dv0 + uL - 1)], inc=sBc)
                emit_lo_batch(uL, hhL % 2, uL % 2)
                rec.emit("scalar", lambda e: e.activation(lnrow[:], lsb[:], AF.Ln),
                         waits=[(sOL, ol0 + 32 * (uL + 1))], inc=sACT)
                rec.emit("scalar", lambda e: e.activation(livrow[:], lnrow[:], AF.Exp, scale=-1.0),
                         inc=sExp2)
                rec.emit("tensor", lambda e: e.matmul(lvb[:], ones_f[:], livrow[:]),
                         waits=[(sExp2, ex20 + uL + 1), (sDVE2, dv0 + uL)], inc=sBc)
                wv = [(sBc, bc0 + uL + 1), (sOL, ol0 + 32 * (uL + 1))]
                wv.append(ctx_st[uL - 2])
                rec.emit("vector", lambda e: e.tensor_copy(livb[:], lvb[:]), waits=wv, inc=sDVE)
                rec.emit("vector", lambda e, p0=uL % 2: e.tensor_mul(ctxb[p0][:], ob[:], livb[:]),
                         inc=sDVE2)
                rec.dma("gpsimd", CTX[jL, :, gL * TSQ:(gL + 1) * TSQ], ctxb[uL % 2][:],
                        S_cst[uL % 2], waits=[(sDVE2, sDVE2.n)])
                ctx_st[uL] = S_cst[uL % 2].done
            barrier()

        if phases >= 7:
            S_ctx = SS("cx", 1)
            S_wo = SS("wo", 4)
            S_sst = SS("ssd", 2)
            with ExitStack() as ph:
                P = ph.enter_context
                ctxr = P(nc.sbuf_tensor("ctxr", [128, HC * TOWN], BF16))
                wob = [P(nc.sbuf_tensor(f"wob{i}", [128, 256], BF16)) for i in range(4)]
                sev = [P(nc.sbuf_tensor(f"sev{i}", [128, 512], F32)) for i in range(8)]
                sbanks = [P(nc.psum_tensor(f"sb{i}", [128, 512], F32)) for i in range(8)]
                for j in range(HC):
                    rec.dma("sync", ctxr[:, j * TOWN:(j + 1) * TOWN], CTX[j], S_ctx[0])
                ld = S_ctx[0].done
                wo_mm_end, wo_ld, dp_dve_end, dp_st_end = {}, {}, {}, {}
                for i in range(4):
                    rec.dma("gpsimd", wob[i][:], woX[i // HC, i % HC], S_wo[i])
                    wo_ld[i] = S_wo[i].done
                for dp in range(16):
                    for hid in range(HC):
                        i = dp * HC + hid
                        wt = [wo_ld[i]]
                        if hid == 0:
                            if dp == 0:
                                wt.append(ld)
                            else:
                                wt.append((sDVE, dp_dve_end[dp - 1]))
                        for pos in range(8):
                            ds, tsub = pos // 4, pos % 4
                            rec.emit("tensor",
                                     lambda e, b=sbanks[pos], wr=wob[i % 4], d0=ds, t0=tsub, h0=hid: e.matmul(
                                         b[:], wr[:, d0 * 128:(d0 + 1) * 128],
                                         ctxr[:, h0 * TOWN + t0 * 512:h0 * TOWN + (t0 + 1) * 512],
                                         start=(h0 == 0), stop=(h0 == HC - 1)),
                                     waits=wt if pos == 0 else (), inc=sPE)
                        wo_mm_end[i] = sPE.n
                        if i + 4 < 16 * HC:
                            i2 = i + 4
                            rec.dma("gpsimd", wob[i2 % 4][:], woX[i2 // HC, i2 % HC], S_wo[i2 % 4],
                                    waits=[(sPE, wo_mm_end[i])])
                            wo_ld[i2] = S_wo[i2 % 4].done
                    cpe = sPE.n
                    wv = [(sPE, cpe)]
                    if dp >= 1:
                        wv.append(dp_st_end[dp - 1])
                    for pos in range(8):
                        ds, tsub = pos // 4, pos % 4
                        dc = dp * 2 + ds
                        rec.emit("vector", lambda e, o=sev[pos], b=sbanks[pos]: e.tensor_copy(o[:], b[:]),
                                 waits=wv if pos == 0 else (), inc=sDVE)
                        rec.dma("scalar", Ss[dc, :, tsub * 512:(tsub + 1) * 512], sev[pos][:],
                                S_sst[dp % 2], waits=[(sDVE, sDVE.n)])
                    dp_dve_end[dp] = sDVE.n
                    dp_st_end[dp] = S_sst[dp % 2].done
            barrier()

        if phases >= 8:
            S_w2 = SS("w2", 2)
            S_hb = SS("hbd", 2)
            S_sx = SS("sx", 2)
            S_out = SS("ou", 2)
            with ExitStack() as ph:
                P = ph.enter_context
                acc2 = [P(nc.sbuf_tensor(f"acc{i}", [128, 4 * TOWN], F32)) for i in range(2)]
                w2b = [P(nc.sbuf_tensor(f"w2b{i}", [128, 512], BF16)) for i in range(16)]
                hb = [P(nc.sbuf_tensor(f"hb{i}", [128, TOWN], BF16)) for i in range(16)]
                f2banks = [P(nc.psum_tensor(f"f2b{i}", [128, 512], F32)) for i in range(8)]
                sld = [P(nc.sbuf_tensor(f"sld{i}", [128, 512], F32)) for i in range(2)]
                xld = [P(nc.sbuf_tensor(f"xld{i}", [128, 512], F32)) for i in range(2)]
                yb2 = P(nc.sbuf_tensor("yb2", [128, 512], F32))
                sg2 = P(nc.sbuf_tensor("sg2", [128, 512], F32))
                gb2 = P(nc.sbuf_tensor("gb2", [128, 512], F32))
                tb2 = P(nc.sbuf_tensor("tb2", [128, 512], F32))
                ob2 = [P(nc.sbuf_tensor(f"ob2{i}", [128, 512], F32)) for i in range(2)]

                prev_fg_acc = None
                prev_fg_mm = None
                pending = []
                prev_batch = None

                def make_final(k, it, cv_all, batch, pb):
                    ds, tsub = it // 4, it % 4
                    dc = k * 4 + ds
                    p = it % 2
                    tsl = slice(tsub * 512, (tsub + 1) * 512)
                    acc = acc2[k % 2]

                    def emit_one():
                        if it >= 2:
                            wl = [(sDVE, batch[it - 2])]
                        elif pb is not None:
                            wl = [(sDVE, pb[14 + it])]
                        else:
                            wl = []
                        rec.dma("scalar", sld[p][:], Ss[dc, :, tsl], S_sx[p], waits=wl)
                        rec.dma("scalar", xld[p][:], xTown[dc * 128:(dc + 1) * 128, tsl], S_sx[p])
                        lds = S_sx[p].done
                        asl = slice(ds * TOWN + tsub * 512, ds * TOWN + (tsub + 1) * 512)
                        rec.emit("vector",
                                 lambda e, s=asl, dc0=dc: e.tensor_scalar(
                                     yb2[:], acc[:, s], b2s[:, dc0:dc0 + 1], None, ALU.add),
                                 waits=[lds, (sDVE, cv_all), init_done], inc=sDVE)
                        cv = sDVE.n
                        rec.emit("scalar", lambda e: e.activation(sg2[:], yb2[:], AF.Sigmoid),
                                 waits=[(sDVE, cv)], inc=sACT)
                        ca = sACT.n
                        rec.emit("vector", lambda e: e.tensor_mul(gb2[:], yb2[:], sg2[:]),
                                 waits=[(sACT, ca)], inc=sDVE)
                        rec.emit("vector", lambda e, p0=p: e.tensor_mul(tb2[:], gb2[:], sld[p0][:]), inc=sDVE)
                        wo = [(S_out[p], S_out[p].n)]
                        rec.emit("vector", lambda e, p0=p: e.tensor_add(ob2[p0][:], tb2[:], xld[p0][:]),
                                 waits=wo, inc=sDVE)
                        batch[it] = sDVE.n
                        rec.dma("scalar", out[dc, :, tsl], ob2[p][:], S_out[p],
                                waits=[(sDVE, sDVE.n)])
                    return emit_one

                for k in range(8):
                    fg_mm_end, fg_acc, w2_ld, hb_ld = {}, {}, {}, {}
                    for fg in range(2):
                        qgate = [(sPE, prev_fg_mm[14 + fg])] if k >= 1 else []
                        for fi in range(8):
                            ffc = fg * 8 + fi
                            slot = (fg % 2) * 8 + fi
                            rec.dma("gpsimd", w2b[slot][:], w2X[k, ffc], S_w2[fg % 2],
                                    waits=qgate if fi == 0 else ())
                            rec.dma("sync", hb[slot][:], Hs[ffc], S_hb[fg % 2],
                                    waits=qgate if fi == 0 else ())
                        w2_ld[fg] = S_w2[fg % 2].done
                        hb_ld[fg] = S_hb[fg % 2].done
                    for fg in range(16):
                        wt = [w2_ld[fg], hb_ld[fg]]
                        for pos in range(16):
                            ds, tsub = pos // 4, pos % 4
                            bk = f2banks[pos % 8]
                            wp = list(wt) if pos == 0 else []
                            if fg > 0 or pos >= 8:
                                prev = (fg, pos - 8) if pos >= 8 else (fg - 1, pos + 8)
                                wp.append((sDVE, fg_acc[prev]))
                            elif k >= 1:
                                wp.append((sDVE, prev_fg_acc[(15, pos + 8)]))
                            for fi in range(8):
                                slot = (fg % 2) * 8 + fi
                                rec.emit("tensor",
                                         lambda e, b=bk, wr=w2b[slot], h_=hb[slot], d0=ds, t0=tsub, f0=fi: e.matmul(
                                             b[:], wr[:, d0 * 128:(d0 + 1) * 128],
                                             h_[:, t0 * 512:(t0 + 1) * 512],
                                             start=(f0 == 0), stop=(f0 == 7)),
                                         waits=wp if fi == 0 else (), inc=sPE)
                            cpe = sPE.n
                            a = acc2[k % 2]
                            sl = slice(ds * TOWN + tsub * 512, ds * TOWN + (tsub + 1) * 512)
                            if fg == 0:
                                rec.emit("vector",
                                         lambda e, b=bk, s=sl, a0=a: e.tensor_copy(a0[:, s], b[:]),
                                         waits=[(sPE, cpe)], inc=sDVE)
                            else:
                                rec.emit("vector",
                                         lambda e, b=bk, s=sl, a0=a: e.tensor_add(a0[:, s], a0[:, s], b[:]),
                                         waits=[(sPE, cpe)], inc=sDVE)
                            fg_acc[(fg, pos)] = sDVE.n
                        fg_mm_end[fg] = sPE.n
                        if pending:
                            pending.pop(0)()
                        if fg + 2 < 16:
                            gw = [(sPE, fg_mm_end[fg])]
                            for fi in range(8):
                                ffc = (fg + 2) * 8 + fi
                                slot = (fg % 2) * 8 + fi
                                rec.dma("gpsimd", w2b[slot][:], w2X[k, ffc], S_w2[fg % 2],
                                        waits=gw if fi == 0 else ())
                                rec.dma("sync", hb[slot][:], Hs[ffc], S_hb[fg % 2],
                                        waits=gw if fi == 0 else ())
                            w2_ld[fg + 2] = S_w2[fg % 2].done
                            hb_ld[fg + 2] = S_hb[fg % 2].done
                    cv_all = sDVE.n
                    batch = {}
                    pending = [make_final(k, it, cv_all, batch, prev_batch) for it in range(16)]
                    prev_batch = batch
                    prev_fg_acc = fg_acc
                    prev_fg_mm = fg_mm_end
                for fn in pending:
                    fn()
            barrier()

        S_tap = SS("tp", 1)
        taps = {"XN": XN, "KT": KT, "VT": VT, "Q": Qs, "CTX": CTX, "S": Ss, "H": Hs}
        for name in debug_taps:
            src = taps[name]
            dt = nc.dram_tensor(f"tap_{name}", list(src.shape), src.dtype, kind="ExternalOutput")
            for i in range(src.shape[0]):
                rec.dma("sync", dt[i], src[i], S_tap[0])
        fin = [(s, s.n) for s in dma_sems if s.n > 0]
        rec.emit("sync", lambda e: e.nop(), waits=fin)

        with nc.Block() as block:
            @block.sync
            def _(e):
                rec.play("sync", e)

            @block.gpsimd
            def _(e):
                rec.play("gpsimd", e)

            @block.tensor
            def _(e):
                rec.play("tensor", e)

            @block.vector
            def _(e):
                rec.play("vector", e)

            @block.scalar
            def _(e):
                rec.play("scalar", e)

    return nc


def host_inputs(full):
    x = np.asarray(full["L0_attn_hidden_states"])
    Wq = np.asarray(full["L0_attn_Wq"])
    Wk = np.asarray(full["L0_attn_Wk"])
    Wv = np.asarray(full["L0_attn_Wv"])
    bv = np.asarray(full["L0_attn_bv"])
    Wo = np.asarray(full["L0_attn_Wo"])
    W1 = np.asarray(full["L0_ffn_W1"])
    b1 = np.asarray(full["L0_ffn_fc1_bias"])
    W2 = np.asarray(full["L0_ffn_W2"])
    b2 = np.asarray(full["L0_ffn_fc2_bias"])

    wkX = np.ascontiguousarray(Wk.reshape(KH, 128, DMC, 128).transpose(0, 3, 2, 1)).reshape(KH, 128, DMC * 128)
    wvX = np.ascontiguousarray(Wv.reshape(KH, 128, DMC, 128).transpose(0, 3, 2, 1)).reshape(KH, 128, DMC * 128)
    wqX = np.ascontiguousarray(Wq.reshape(HC, 128, DMC, 128).transpose(0, 3, 2, 1)).reshape(HC, 128, DMC * 128)
    woX = np.ascontiguousarray(Wo.reshape(16, 256, HC, 128).transpose(0, 2, 3, 1))
    w1X = np.ascontiguousarray(W1.reshape(DMC, 128, FFC, 128).transpose(2, 1, 0, 3)).reshape(FFC, 128, DMC * 128)
    w2X = np.ascontiguousarray(W2.reshape(FFC, 128, 8, 512).transpose(2, 0, 1, 3))

    shared = {
        "wkX": wkX, "wvX": wvX, "wqX": wqX, "woX": woX, "w1X": w1X, "w2X": w2X,
        "b1X": np.ascontiguousarray(b1.reshape(FFC, 128).T),
        "b2X": np.ascontiguousarray(b2.reshape(DMC, 128).T),
        "bvX": np.ascontiguousarray(bv.reshape(KH, 128).T),
        "onesbf": np.ones((128, 1), dtype=BF),
        "ones1f": np.ones((1, 128), dtype=np.float32),
        "identbf": np.eye(128, dtype=BF),
    }
    per_core = []
    for c in range(8):
        b, own = c // 2, c % 2
        xb = np.ascontiguousarray(x[b].T)
        if own == 1:
            xb = np.ascontiguousarray(np.concatenate([xb[:, TOWN:], xb[:, :TOWN]], axis=1))
        m = dict(shared)
        m["xT"] = xb
        m["xTown"] = np.ascontiguousarray(xb[:, 0:TOWN])
        per_core.append(m)
    return per_core


def kernel(**inputs):
    per_core = host_inputs(inputs)
    nc = build_program()
    res = run_bass_kernel_spmd(nc, per_core, core_ids=list(range(8)))
    outs = []
    for c in range(8):
        o = np.asarray(res.results[c]["out"])
        outs.append(o.reshape(D, TOWN))
    full = np.empty((4, T, D), dtype=np.float32)
    for b in range(4):
        oT = np.concatenate([outs[2 * b], outs[2 * b + 1]], axis=1)
        full[b] = oT.T
    return full
